# version 13
# baseline (speedup 1.0000x reference)
import numpy as np
import ml_dtypes
from contextlib import ExitStack

import concourse.bass as bass
import concourse.mybir as mybir
from concourse.bass_utils import run_bass_kernel_spmd

F32 = mybir.dt.float32
BF16 = mybir.dt.bfloat16
U32 = mybir.dt.uint32
AF = mybir.ActivationFunctionType
ALU = mybir.AluOpType
AX = mybir.AxisListType.X

T = 2048
H = 1024
DI = 2048
NST = 16
RK = 64
E = 8
TT = 16
HT = 8
DIT = 16
NSC = 256
LN_EPS = 1e-5
G_PAIRS = 8
N_GROUPS = 4

ENGINES = ("sy", "gp", "ve", "se", "te")


class Prog:
    def __init__(self, nc, stack):
        self.nc = nc
        self._stack = stack
        self.q = {e: [] for e in ENGINES}
        self.sems = {}

    def sem(self, name, stack):
        h = stack.enter_context(self.nc.semaphore(name))
        self.sems[name] = [h, 0]

    def val(self, name):
        return self.sems[name][1]

    def wait(self, eng, sem, val):
        if val <= 0:
            return
        h = self.sems[sem][0]
        self.q[eng].append(lambda EE, h=h, v=val: EE.wait_ge(h, v))

    def op(self, eng, fn, inc=None, n=1):
        if inc is None:
            self.q[eng].append(fn)
            return None
        h = self.sems[inc][0]
        self.sems[inc][1] += n
        self.q[eng].append(lambda EE, fn=fn, h=h, n=n: fn(EE).then_inc(h, n))
        return self.sems[inc][1]

    def dma(self, eng, out_ap, in_ap, stream):
        if stream not in self.sems:
            h = self._stack.enter_context(self.nc.semaphore(stream))
            self.sems[stream] = [h, 0]
        self.op(eng, lambda EE: EE.dma_start(out_ap, in_ap), inc=stream, n=16)
        return (stream, self.sems[stream][1])

    def tok(self, stream):
        if stream not in self.sems:
            return None
        return (stream, self.sems[stream][1])

    def wdma(self, eng, tok):
        if tok is None:
            return
        stream, v = tok
        self.wait(eng, stream, v)

    def sop(self, eng, fn):
        sem = {"ve": "VE", "se": "SE"}[eng]
        self.wait(eng, sem, self.val(sem))
        return self.op(eng, fn, inc=sem)


def build_nc():
    nc = bass.Bass("TRN2", target_bir_lowering=False)

    d_x = nc.dram_tensor("x", [T, H], F32, kind="ExternalInput").ap()
    d_winT = nc.dram_tensor("winT", [33, H, 128], F32, kind="ExternalInput").ap()
    d_wdtT = nc.dram_tensor("wdtT", [DIT, RK, 128], F32, kind="ExternalInput").ap()
    d_wguT = nc.dram_tensor("wguT", [E, 32, 2, H, 128], F32, kind="ExternalInput").ap()
    d_wdT = nc.dram_tensor("wdT", [E, 32, 128, H], F32, kind="ExternalInput").ap()
    d_woutT = nc.dram_tensor("woutT", [DIT, 128, H], F32, kind="ExternalInput").ap()
    d_gwT = nc.dram_tensor("gwT", [HT, 128, E], F32, kind="ExternalInput").ap()
    d_alogr = nc.dram_tensor("alogr", [128, NSC], F32, kind="ExternalInput").ap()
    d_bdt = nc.dram_tensor("bdt", [128, DIT], F32, kind="ExternalInput").ap()
    d_dcol = nc.dram_tensor("dcol", [128, DIT], F32, kind="ExternalInput").ap()
    d_cvw = nc.dram_tensor("cvw", [128, 64], F32, kind="ExternalInput").ap()
    d_cvb = nc.dram_tensor("cvb", [128, DIT], F32, kind="ExternalInput").ap()
    d_lng = nc.dram_tensor("lng", [128, H], F32, kind="ExternalInput").ap()
    d_lnb = nc.dram_tensor("lnb", [128, H], F32, kind="ExternalInput").ap()
    d_identf = nc.dram_tensor("identf", [128, 128], F32, kind="ExternalInput").ap()
    d_sel = nc.dram_tensor("sel", [128, 16 * 128], BF16, kind="ExternalInput").ap()
    d_out = nc.dram_tensor("out", [T, H], F32, kind="ExternalOutput").ap()

    s_dtdu = nc.dram_tensor("s_dtdu", [DIT, 128, 2 * T], BF16, kind="Internal").ap()
    s_sz = nc.dram_tensor("s_sz", [DIT, 128, T], BF16, kind="Internal").ap()
    s_w = nc.dram_tensor("s_w", [DIT, 128, T], BF16, kind="Internal").ap()
    s_y = nc.dram_tensor("s_y", [DIT, 128, T], BF16, kind="Internal").ap()
    s_bc = nc.dram_tensor("s_bc", [32, T], F32, kind="Internal").ap()
    s_wgu16 = nc.dram_tensor("s_wgu16", [E, 32, 2, H, 128], BF16, kind="Internal").ap()
    s_wd16 = nc.dram_tensor("s_wd16", [E, 32, 128, H], BF16, kind="Internal").ap()

    with ExitStack() as ctx:
        P = Prog(nc, ctx)
        for s in ("VE", "SE", "TE", "GP"):
            P.sem(s, ctx)

        sbuf = lambda n, sh, dt=F32: ctx.enter_context(nc.sbuf_tensor(n, sh, dt))

        xnT16 = sbuf("xnT16", [128, HT * T], BF16)
        af32 = sbuf("af32", [128, 16 * 1024])
        abf = sbuf("abf", [128, 9 * T], BF16)
        warena = sbuf("warena", [128, 8 * T], BF16)
        marena = sbuf("marena", [128, 4 * 1024], BF16)
        wdb = sbuf("wdb", [128, 8 * 1024], BF16)
        gelb16 = sbuf("gelb16", [128, 1024], BF16)
        c_lng = sbuf("c_lng", [128, H], BF16)
        c_lnb = marena[:, 0:1024]
        c_sel = sbuf("c_sel", [128, 16 * 128], BF16)
        c_acols = sbuf("c_acols", [128, NSC])
        c_rstd = sbuf("c_rstd", [128, TT * 8])
        c_cmb16 = sbuf("c_cmb16", [128, TT * E], BF16)
        sm8 = sbuf("sm8", [128, 8 * 8])
        c_identf = gelb16[:, 0:256].bitcast(F32)
        stage = gelb16[:, 256:512].bitcast(F32)
        c_gw = wdb[:, 0:128].bitcast(F32)
        c_bdt = wdb[:, 128:160].bitcast(F32)
        c_dcol = wdb[:, 160:192].bitcast(F32)
        c_cvw = wdb[:, 192:320].bitcast(F32)
        c_cvb = wdb[:, 320:352].bitcast(F32)
        smE = sbuf("smE", [128, 5 * 8])
        bhh = sbuf("bhh", [128, T], BF16)
        chh = sbuf("chh", [128, T], BF16)

        psA = ctx.enter_context(nc.psum_tensor("psA", [128, T], F32))
        psB = ctx.enter_context(nc.psum_tensor("psB", [128, T], F32))

        SL = lambda s: abf[:, s * T:(s + 1) * T]

        blk = ctx.enter_context(nc.Block())

        P.dma("sy", c_identf, d_identf, "C")
        P.dma("sy", c_sel[:], d_sel, "C")
        P.dma("sy", af32[:, 8192:8448], d_alogr, "C")
        P.dma("sy", c_bdt[:], d_bdt, "C")
        P.dma("sy", c_dcol[:], d_dcol, "C")
        P.dma("sy", c_cvw[:], d_cvw, "C")
        P.dma("sy", c_cvb[:], d_cvb, "C")

        v_consts = P.dma("sy", c_gw.rearrange("p (j e) -> p j e", j=HT),
                         d_gwT.rearrange("j p e -> p j e"), "C")
        v_c2 = P.dma("gp", c_lng[:], d_lng, "C2")
        P.wdma("se", v_consts)
        P.op("se", lambda EE: EE.activation(af32[:, 8192:8448], af32[:, 8192:8448], AF.Exp), inc="SE")
        v_acols_se = P.val("SE")
        P.wait("ve", "SE", v_acols_se)
        P.op("ve", lambda EE: EE.tensor_scalar_mul(c_acols[:], af32[:, 8192:8448], -1.0))
        P.op("ve", lambda EE: EE.memset(smE[:, 0:8], 1.0))
        P.op("ve", lambda EE: EE.memset(smE[:, 8:16], -1e30))
        P.op("ve", lambda EE: EE.memset(smE[:, 16:24], 0.0))
        P.op("ve", lambda EE: EE.memset(smE[:, 24:32], LN_EPS))
        P.op("ve", lambda EE: EE.memset(smE[:, 32:40], -0.5), inc="VE")
        v_acols = P.val("VE")

        cv_tok = [None] * E

        def conv_track():
            for e in range(E):
                for p0 in range(0, 32, 8):
                    P.dma("gp", s_wgu16[e, p0:p0 + 8], d_wguT[e, p0:p0 + 8], "CV")
                for f0 in range(0, 32, 16):
                    cv_tok[e] = P.dma("gp", s_wd16[e, f0:f0 + 16], d_wdT[e, f0:f0 + 16], "CV")
                yield

        conv_gen = conv_track()

        def pull_conv(n):
            for _ in range(n):
                next(conv_gen, None)

        pull_conv(2)

        ev_vals = []
        rmm_vals = []
        x_rel = []
        for i in range(TT):
            slot = i % 2
            xv = af32[:, slot * H:(slot + 1) * H]
            if i >= 2:
                P.wait("sy", "TE", x_rel[i - 2])
            vx = P.dma("sy", xv, d_x.rearrange("(i p) h -> i p h", p=128)[i], "X%d" % slot)

            base = 4096 + slot * 64
            sm1 = af32[:, base + 0:base + 1]
            ssqv = af32[:, base + 1:base + 2]
            mu8 = af32[:, base + 8:base + 16]
            q8 = af32[:, base + 16:base + 24]
            m28 = af32[:, base + 24:base + 32]
            lnv8 = af32[:, base + 32:base + 40]
            P.wdma("ve", vx)
            P.op("ve", lambda EE, sm1=sm1, xv=xv: EE.reduce_sum(sm1, xv, axis=AX), inc="VE")
            v_mu = P.val("VE")
            sq = af32[:, 4224:4224 + H]
            P.wdma("se", vx)
            P.wait("se", "VE", v_mu)
            P.op("se", lambda EE, sq=sq, xv=xv, ssqv=ssqv:
                 EE.activation(sq, xv, AF.Square, accum_out=ssqv), inc="SE")
            v_ssq = P.val("SE")
            P.wait("ve", "SE", v_ssq)
            P.op("ve", lambda EE, mu8=mu8, sm1=sm1:
                 EE.tensor_scalar(mu8, smE[:, 0:8], sm1, 1.0 / H, ALU.mult, ALU.mult))
            P.op("ve", lambda EE, q8=q8, ssqv=ssqv:
                 EE.tensor_scalar(q8, smE[:, 0:8], ssqv, 1.0 / H, ALU.mult, ALU.mult), inc="VE")
            P.sop("ve", lambda EE, m28=m28, mu8=mu8: EE.tensor_tensor(m28, mu8, mu8, ALU.mult))
            var8 = af32[:, base + 40:base + 48]
            P.sop("ve", lambda EE, var8=var8, q8=q8, m28=m28:
                 EE.tensor_tensor(var8, q8, m28, ALU.subtract))
            v_var = P.val("VE")
            rstd8 = c_rstd[:, i * 8:(i + 1) * 8]
            nmr8 = af32[:, base + 48:base + 56]
            P.wait("se", "VE", v_var)
            P.sop("se", lambda EE, lnv8=lnv8, var8=var8:
                 EE.activation(lnv8, var8, AF.Ln, bias=smE[:, 24:25]))
            P.sop("se", lambda EE, rstd8=rstd8, lnv8=lnv8:
                 EE.activation(rstd8, lnv8, AF.Exp, scale=smE[:, 32:33]))
            v_rstd = P.val("SE")
            P.wait("ve", "SE", v_rstd)
            P.op("ve", lambda EE, nmr8=nmr8, mu8=mu8, rstd8=rstd8:
                 EE.tensor_scalar(nmr8, mu8, rstd8[:, 0:1], -1.0, ALU.mult, ALU.mult), inc="VE")
            v_nmr = P.val("VE")

            xnv = af32[:, 2048 + slot * H: 2048 + (slot + 1) * H]
            P.wait("se", "VE", v_nmr)
            if i >= 2:
                P.wait("se", "TE", x_rel[i - 2])
            P.op("se", lambda EE, xnv=xnv, xv=xv, rstd8=rstd8, nmr8=nmr8:
                 EE.activation(xnv, xv, AF.Identity, scale=rstd8[:, 0:1], bias=nmr8[:, 0:1]), inc="SE")
            v_xn = P.val("SE")

            P.wait("te", "SE", v_xn)
            if i == 0:
                P.wdma("te", v_consts)
                P.wait("te", "VE", v_acols)
            for j in range(HT):
                idx = i * HT + j
                pst = psA[:, (j % 2) * 512:(j % 2) * 512 + 128]
                if idx >= 2:
                    P.wait("te", "VE", ev_vals[idx - 2])
                P.op("te", lambda EE, pst=pst, xnv=xnv, j=j:
                     EE.transpose(pst, xnv[:, j * 128:(j + 1) * 128], c_identf), inc="TE")
                v_tr = P.val("TE")
                stg = stage[:, 0:128]
                P.wait("ve", "TE", v_tr)
                if idx >= 1:
                    P.wait("ve", "TE", rmm_vals[idx - 1])
                P.op("ve", lambda EE, stg=stg, pst=pst: EE.tensor_copy(stg, pst))
                P.op("ve", lambda EE, pst=pst, j=j, i=i:
                     EE.tensor_copy(xnT16[:, j * T + i * 128: j * T + (i + 1) * 128], pst),
                     inc="VE")
                ev_vals.append(P.val("VE"))
                P.wait("te", "VE", ev_vals[-1])
                P.op("te", lambda EE, stg=stg, j=j:
                     EE.matmul(psB[:, 0:8], stg, c_gw[:, j * E:(j + 1) * E],
                               start=(j == 0), stop=(j == HT - 1)), inc="TE")
                rmm_vals.append(P.val("TE"))
            x_rel.append(P.val("TE"))
            v_rmm = P.val("TE")

            lg = sm8[:, 0:8]; mx = sm8[:, 8:16]; nx8 = sm8[:, 16:24]; ex = sm8[:, 24:32]
            m1 = sm8[:, 32:40]; mk = sm8[:, 40:48]; m2 = sm8[:, 48:56]
            e1 = sm8[:, 56:64]
            e2 = sm8[:, 16:24]
            ws = sm8[:, 8:16]
            rw = sm8[:, 8:16]
            w1 = sm8[:, 0:8]
            w2 = sm8[:, 24:32]
            tmp8b = sm8[:, 48:56]
            cmbv = c_cmb16[:, i * E:(i + 1) * E]
            P.wait("ve", "TE", v_rmm)
            P.sop("ve", lambda EE, lg=lg: EE.tensor_copy(lg, psB[:, 0:8]))
            P.sop("ve", lambda EE, mx=mx, lg=lg: EE.reduce_max(mx[:, 0:1], lg, axis=AX))
            P.sop("ve", lambda EE, nx8=nx8, mx=mx:
                  EE.tensor_scalar(nx8, smE[:, 0:8], mx[:, 0:1], -1.0, ALU.mult, ALU.mult))
            P.wait("se", "VE", P.val("VE"))
            P.sop("se", lambda EE, ex=ex, lg=lg, nx8=nx8:
                  EE.activation(ex, lg, AF.Exp, bias=nx8[:, 0:1]))
            P.wait("ve", "SE", P.val("SE"))
            P.sop("ve", lambda EE, m1=m1, ex=ex: EE.reduce_max(m1[:, 0:1], ex, axis=AX))
            P.sop("ve", lambda EE, e1=e1, ex=ex, m1=m1:
                  EE.tensor_single_scalar(e1, ex, m1[:, 0:1], ALU.is_ge))
            P.sop("ve", lambda EE, mk=mk, e1=e1, ex=ex:
                  EE.select(mk, e1.bitcast(U32), smE[:, 8:16], ex))
            P.sop("ve", lambda EE, m2=m2, mk=mk: EE.reduce_max(m2[:, 0:1], mk, axis=AX))
            P.sop("ve", lambda EE, e2=e2, mk=mk, m2=m2:
                  EE.tensor_single_scalar(e2, mk, m2[:, 0:1], ALU.is_ge))
            P.sop("ve", lambda EE, ws=ws, m1=m1:
                  EE.tensor_scalar(ws, smE[:, 0:8], m1[:, 0:1], None, ALU.mult))
            P.sop("ve", lambda EE, mk=mk, ws=ws, m2=m2:
                  EE.tensor_single_scalar(mk, ws, m2[:, 0:1], ALU.add))
            P.sop("ve", lambda EE, rw=rw, mk=mk: EE.reciprocal(rw, mk))
            P.sop("ve", lambda EE, w1=w1, rw=rw, m1=m1:
                  EE.tensor_single_scalar(w1, rw, m1[:, 0:1], ALU.mult))
            P.sop("ve", lambda EE, w2=w2, rw=rw, m2=m2:
                  EE.tensor_single_scalar(w2, rw, m2[:, 0:1], ALU.mult))
            P.sop("ve", lambda EE, tmp8b=tmp8b, e2=e2, w2=w2:
                  EE.select(tmp8b, e2.bitcast(U32), w2, smE[:, 16:24]))
            P.sop("ve", lambda EE, cmbv=cmbv, e1=e1, w1=w1, tmp8b=tmp8b:
                  EE.select(cmbv, e1.bitcast(U32), w1, tmp8b))
            if i % 3 == 2:
                pull_conv(1)

        win_ring = [[0, 0] for _ in range(4)]

        def plan_win_load(ot, slot):
            P.wait("gp", "TE", win_ring[slot][1])
            ov = marena[:, slot * 1024:(slot + 1) * 1024]
            v = P.dma("gp", ov.rearrange("p (j m) -> p j m", j=HT),
                      d_winT[ot].rearrange("(j p) m -> p j m", p=128), "WIN%d" % slot)
            win_ring[slot][0] = v
            return ov

        w32 = plan_win_load(32, 0)
        P.wdma("te", win_ring[0][0])
        P.wait("te", "VE", ev_vals[-1])
        for c in range(4):
            for j in range(HT):
                P.op("te", lambda EE, c=c, j=j, w32=w32:
                     EE.matmul(psA[0:32, c * 512:(c + 1) * 512],
                               w32[:, j * 128:j * 128 + 32],
                               xnT16[:, j * T + c * 512: j * T + (c + 1) * 512],
                               start=(j == 0), stop=(j == HT - 1)),
                     inc=("TE" if (c == 3 and j == HT - 1) else None))
        v_bcmm = P.val("TE")
        for c in range(4):
            for j in range(HT):
                P.op("te", lambda EE, c=c, j=j, w32=w32:
                     EE.matmul(psB[0:64, c * 512:(c + 1) * 512],
                               w32[:, j * 128 + 32:j * 128 + 96],
                               xnT16[:, j * T + c * 512: j * T + (c + 1) * 512],
                               start=(j == 0), stop=(j == HT - 1)),
                     inc=("TE" if (c == 3 and j == HT - 1) else None))
        v_o32 = P.val("TE")
        win_ring[0][1] = v_o32

        bc_sb = af32[0:32, 0:T]
        dtp16 = abf[0:64, 0:T]
        P.wait("ve", "TE", v_o32)
        P.op("ve", lambda EE: EE.tensor_copy(bc_sb, psA[0:32, :]), inc="VE")
        v_bc = P.val("VE")
        P.op("ve", lambda EE: EE.tensor_copy(dtp16, psB[0:64, :]), inc="VE")
        v_dtp = P.val("VE")
        P.wait("sy", "VE", v_bc)
        v_bc_st = P.dma("sy", s_bc, bc_sb, "BCS")
        P.wdma("gp", v_bc_st)
        v_bh = P.dma("gp", bhh[:], s_bc[0:16].unsqueeze(0).broadcast_to([8, 16, T]), "BH")
        v_ch = P.dma("gp", chh[:], s_bc[16:32].unsqueeze(0).broadcast_to([8, 16, T]), "BH")

        wdt_sb = warena[0:64, 0:T]
        v_wdt = P.dma("gp", wdt_sb.rearrange("r (k m) -> r k m", k=DIT),
                      d_wdtT.rearrange("k r m -> r k m"), "WDT")
        dt_se, dt_ve, dt_spill = [], [], []
        for k in range(DIT):
            pdt = psA if k % 2 == 0 else psB
            if k == 0:
                P.wdma("te", v_wdt)
                P.wait("te", "VE", v_bc)
            if k == 1:
                P.wait("te", "VE", v_dtp)
            if k >= 2:
                P.wait("te", "SE", dt_se[k - 2])
            for c in range(4):
                P.op("te", lambda EE, pdt=pdt, k=k, c=c:
                     EE.matmul(pdt[:, c * 512:(c + 1) * 512],
                               wdt_sb[:, k * 128:(k + 1) * 128],
                               dtp16[:, c * 512:(c + 1) * 512],
                               start=True, stop=True),
                     inc=("TE" if c == 3 else None))
            v_mm = P.val("TE")
            rbuf = af32[:, 10368:12416]
            abuf = af32[:, 4224:6272]
            bcol = c_bdt[:, k:k + 1]
            P.wait("se", "TE", v_mm)
            if k >= 1:
                P.wait("se", "VE", dt_ve[k - 1])
            P.op("se", lambda EE, rbuf=rbuf, pdt=pdt, bcol=bcol:
                 EE.activation(rbuf, pdt[:], AF.Relu, bias=bcol))
            P.op("se", lambda EE, abuf=abuf, pdt=pdt, bcol=bcol:
                 EE.activation(abuf, pdt[:], AF.Abs, bias=bcol), inc="SE")
            dt_se.append(P.val("SE"))
            P.op("se", lambda EE, abuf=abuf: EE.activation(abuf, abuf, AF.Exp, scale=-1.0))
            P.op("se", lambda EE, abuf=abuf: EE.activation(abuf, abuf, AF.Ln, bias=1.0), inc="SE")
            v_spl = P.val("SE")
            dt16 = SL(1 + k % 2)
            P.wait("ve", "SE", v_spl)
            if k >= 2:
                P.wdma("ve", dt_spill[k - 2])
            P.op("ve", lambda EE, dt16=dt16, rbuf=rbuf, abuf=abuf:
                 EE.tensor_tensor(dt16, rbuf, abuf, ALU.add), inc="VE")
            dt_ve.append(P.val("VE"))
            P.wait("sy", "VE", dt_ve[-1])
            dt_spill.append(P.dma("sy", s_dtdu[k][:, 0:T], dt16, "DS%d" % (k % 2)))
        v_dt_se_all = P.val("SE")

        pair_conv, pair_silu, pair_ve, pair_spill = [], [], [], []
        for k in range(DIT):
            sx = (2 * k) % 4
            sz_ = (2 * k + 1) % 4
            wx = plan_win_load(k, sx)
            wz = plan_win_load(16 + k, sz_)
            P.wdma("te", win_ring[sx][0])
            P.wdma("te", win_ring[sz_][0])
            if k == 0:
                P.wait("te", "SE", v_dt_se_all)
            else:
                P.wait("te", "VE", pair_conv[k - 1])
                P.wait("te", "SE", pair_silu[k - 1])
            for c in range(4):
                for j in range(HT):
                    P.op("te", lambda EE, wx=wx, j=j, c=c:
                         EE.matmul(psA[:, c * 512:(c + 1) * 512],
                                   wx[:, j * 128:(j + 1) * 128],
                                   xnT16[:, j * T + c * 512: j * T + (c + 1) * 512],
                                   start=(j == 0), stop=(j == HT - 1)),
                         inc=("TE" if (c == 3 and j == HT - 1) else None))
            v_px = P.val("TE")
            for c in range(4):
                for j in range(HT):
                    P.op("te", lambda EE, wz=wz, j=j, c=c:
                         EE.matmul(psB[:, c * 512:(c + 1) * 512],
                                   wz[:, j * 128:(j + 1) * 128],
                                   xnT16[:, j * T + c * 512: j * T + (c + 1) * 512],
                                   start=(j == 0), stop=(j == HT - 1)),
                         inc=("TE" if (c == 3 and j == HT - 1) else None))
            v_pair = P.val("TE")
            win_ring[sx][1] = v_pair
            win_ring[sz_][1] = v_pair

            xb = af32[:, 2048:2048 + T + 3]
            t1 = af32[:, 6272:8320]
            t2 = af32[:, 8320:10368]
            u32 = af32[:, 2048:4096]
            sz32 = af32[:, 4224:6272]
            P.wait("ve", "TE", v_px)
            if k == 0:
                P.wait("ve", "SE", v_dt_se_all)
            P.op("ve", lambda EE, xb=xb: EE.memset(xb[:, 0:3], 0.0))
            P.op("ve", lambda EE, xb=xb: EE.tensor_copy(xb[:, 3:3 + T], psA[:]))
            P.op("ve", lambda EE, t1=t1, xb=xb, k=k:
                 EE.tensor_single_scalar(t1, xb[:, 0:T], c_cvw[:, 4 * k:4 * k + 1], ALU.mult))
            P.op("ve", lambda EE, t2=t2, xb=xb, t1=t1, k=k:
                 EE.scalar_tensor_tensor(t2, xb[:, 1:1 + T], c_cvw[:, 4 * k + 1:4 * k + 2],
                                         t1, ALU.mult, ALU.add))
            P.op("ve", lambda EE, t1=t1, xb=xb, t2=t2, k=k:
                 EE.scalar_tensor_tensor(t1, xb[:, 2:2 + T], c_cvw[:, 4 * k + 2:4 * k + 3],
                                         t2, ALU.mult, ALU.add))
            P.op("ve", lambda EE, t2=t2, xb=xb, t1=t1, k=k:
                 EE.scalar_tensor_tensor(t2, xb[:, 3:3 + T], c_cvw[:, 4 * k + 3:4 * k + 4],
                                         t1, ALU.mult, ALU.add), inc="VE")
            pair_conv.append(P.val("VE"))
            P.wait("se", "VE", pair_conv[-1])
            P.wait("se", "TE", v_pair)
            P.op("se", lambda EE, u32=u32, t2=t2, k=k:
                 EE.activation(u32, t2, AF.Silu, bias=c_cvb[:, k:k + 1]), inc="SE")
            P.op("se", lambda EE, sz32=sz32: EE.activation(sz32, psB[:], AF.Silu), inc="SE")
            pair_silu.append(P.val("SE"))

            dt16r = SL(3 + k % 2)
            P.wdma("sy", dt_spill[k])
            if k >= 2:
                P.wait("sy", "VE", pair_ve[k - 2])
            v_dtr = P.dma("sy", dt16r, s_dtdu[k][:, 0:T], "DR%d" % (k % 2))
            du16 = SL(5 + k % 2)
            sz16 = SL(7)
            w16 = SL(8)
            P.wait("ve", "SE", pair_silu[-1])
            P.wdma("ve", v_dtr)
            if k >= 1:
                P.wdma("ve", pair_spill[k - 1][0])
                P.wdma("ve", pair_spill[k - 1][1])
            P.op("ve", lambda EE, du16=du16, dt16r=dt16r, u32=u32:
                 EE.tensor_tensor(du16, dt16r, u32, ALU.mult))
            P.op("ve", lambda EE, sz16=sz16, sz32=sz32: EE.tensor_copy(sz16, sz32))
            P.op("ve", lambda EE, t1=t1, u32=u32, k=k:
                 EE.tensor_single_scalar(t1, u32, c_dcol[:, k:k + 1], ALU.mult))
            P.op("ve", lambda EE, w16=w16, t1=t1, sz32=sz32:
                 EE.tensor_tensor(w16, t1, sz32, ALU.mult), inc="VE")
            pair_ve.append(P.val("VE"))
            P.wait("sy", "VE", pair_ve[-1])
            P.dma("sy", s_dtdu[k][:, T:2 * T], du16, "US%d" % (k % 2))
            v_szst = P.dma("sy", s_sz[k], sz16, "SS")
            v_wst = P.dma("sy", s_w[k], w16, "WS%d" % (k % 2))
            pair_spill.append((v_szst, v_wst))

        v_p1_ve = P.val("VE")
        v_p1_se = P.val("SE")
        v_p1_te = P.val("TE")
        p1_spill_toks = [P.tok(s) for s in
                         ("DS0", "DS1", "US0", "US1", "SS", "WS0", "WS1", "DR0", "DR1")]

        pull_conv(8)

        part = af32[:, 0:TT * H]
        v_p1_ve = P.val("VE")
        v_p1_se = P.val("SE")
        v_p1_te = P.val("TE")

        moe_state = {
            "wgu_rel": [v_p1_te] * 2,
            "wd_rel": v_p1_te,
            "wd_tok": None,
            "hact_rel": [v_p1_te] * G_PAIRS,
            "brel": [0, 0, 0, 0],
            "gel_rel": [0, 0],
            "first": True,
        }

        def moe_track():
            st = moe_state
            wslab = 0
            for e in range(E):
                for g in range(N_GROUPS):
                    if g == 0:
                        P.wdma("gp", cv_tok[e])
                    if e == 0 and g == 0:
                        P.wait("gp", "VE", v_p1_ve)
                    for fj in range(G_PAIRS):
                        pr = g * G_PAIRS + fj
                        ps_ = wslab % 2
                        wslab += 1
                        P.wait("gp", "TE", st["wgu_rel"][ps_])
                        pslot = marena[:, ps_ * 2048:(ps_ + 1) * 2048]
                        vg = P.dma("gp", pslot.rearrange("p (u j m) -> p u j m", u=2, j=HT),
                                   s_wgu16[e, pr].rearrange("u (j p) m -> p u j m", p=128),
                                   "GU%d" % ps_)
                        vu = vg
                        wg = pslot[:, 0:1024]
                        wu = pslot[:, 1024:2048]
                        if fj == 1:
                            P.wait("gp", "TE", st["wd_rel"])
                            for q4 in range(2):
                                st["wd_tok"] = P.dma(
                                    "gp", wdb[:, q4 * 4096:(q4 + 1) * 4096].rearrange(
                                        "p (f h) -> p f h", f=4),
                                    s_wd16[e, g * G_PAIRS + q4 * 4:g * G_PAIRS + (q4 + 1) * 4]
                                    .rearrange("f p h -> p f h"), "WDL")
                        hrow = warena[:, fj * T:(fj + 1) * T]
                        for c in range(4):
                            pc = c % 2
                            psg = psB[:, pc * 512:(pc + 1) * 512]
                            psu = psB[:, (2 + pc) * 512:(3 + pc) * 512]
                            P.wdma("te", vg)
                            P.wdma("te", vu)
                            if st["first"]:
                                P.wait("te", "SE", v_p1_se)
                                P.wait("te", "VE", v_p1_ve)
                                st["first"] = False
                            P.wait("te", "VE", st["brel"][pc])
                            P.wait("te", "VE", st["brel"][2 + pc])
                            for j in range(HT):
                                P.op("te", lambda EE, psg=psg, wg=wg, j=j, c=c:
                                     EE.matmul(psg, wg[:, j * 128:(j + 1) * 128],
                                               xnT16[:, j * T + c * 512: j * T + (c + 1) * 512],
                                               start=(j == 0), stop=(j == HT - 1)),
                                     inc=("TE" if j == HT - 1 else None))
                            v_g = P.val("TE")
                            for j in range(HT):
                                P.op("te", lambda EE, psu=psu, wu=wu, j=j, c=c:
                                     EE.matmul(psu, wu[:, j * 128:(j + 1) * 128],
                                               xnT16[:, j * T + c * 512: j * T + (c + 1) * 512],
                                               start=(j == 0), stop=(j == HT - 1)),
                                     inc=("TE" if j == HT - 1 else None))
                            v_u = P.val("TE")
                            gelb = gelb16[:, pc * 512:(pc + 1) * 512]
                            P.wait("se", "TE", v_g)
                            P.wait("se", "VE", st["gel_rel"][pc])
                            P.op("se", lambda EE, gelb=gelb, psg=psg:
                                 EE.activation(gelb, psg, AF.Gelu), inc="SE")
                            v_gel = P.val("SE")
                            P.wait("ve", "SE", v_gel)
                            P.wait("ve", "TE", v_u)
                            if c == 0:
                                P.wait("ve", "TE", st["hact_rel"][fj])
                            P.op("ve", lambda EE, hrow=hrow, psu=psu, gelb=gelb, c=c:
                                 EE.tensor_tensor(hrow[:, c * 512:(c + 1) * 512], psu, gelb,
                                                  ALU.mult), inc="VE")
                            st["brel"][pc] = P.val("VE")
                            st["brel"][2 + pc] = P.val("VE")
                            st["gel_rel"][pc] = P.val("VE")
                            yield
                        st["wgu_rel"][ps_] = P.val("TE")
                    v_hact_all = P.val("VE")

                    P.wdma("te", st["wd_tok"])
                    P.wait("te", "VE", v_hact_all)
                    for i in range(TT):
                        for hh in range(2):
                            idx = i * 2 + hh
                            psd = psB[:, (idx % 2) * 512:(idx % 2 + 1) * 512]
                            P.wait("te", "VE", st["brel"][idx % 2])
                            for fj in range(G_PAIRS):
                                P.op("te", lambda EE, psd=psd, fj=fj, i=i, hh=hh:
                                     EE.matmul(psd,
                                               warena[:, fj * T + i * 128: fj * T + (i + 1) * 128],
                                               wdb[:, fj * 1024 + hh * 512: fj * 1024 + (hh + 1) * 512],
                                               start=(fj == 0), stop=(fj == G_PAIRS - 1)),
                                     inc=("TE" if fj == G_PAIRS - 1 else None))
                            v_dn = P.val("TE")
                            pv = part[:, i * H + hh * 512: i * H + (hh + 1) * 512]
                            P.wait("ve", "TE", v_dn)
                            P.op("ve", lambda EE, pv=pv, psd=psd, i=i, e=e:
                                 EE.scalar_tensor_tensor(pv, psd[:], c_cmb16[:, i * E + e:i * E + e + 1],
                                                         pv, ALU.mult, ALU.add), inc="VE")
                            st["brel"][idx % 2] = P.val("VE")
                            yield
                    v_grp_te = P.val("TE")
                    moe_state["wd_rel"] = v_grp_te
                    for fj in range(G_PAIRS):
                        moe_state["hact_rel"][fj] = v_grp_te
            while True:
                yield None

        moe_gen = moe_track()

        def pull_moe(n):
            for _ in range(n):
                next(moe_gen)

        p1_spill_toks = [P.tok(s) for s in
                         ("DS0", "DS1", "US0", "US1", "SS", "WS0", "WS1", "DR0", "DR1")]
        for tk in p1_spill_toks:
            P.wdma("sy", tk)
        P.wait("sy", "TE", v_p1_te)
        P.wait("sy", "VE", v_p1_ve)
        P.wait("sy", "SE", v_p1_se)

        P.wait("ve", "SE", v_p1_se)
        P.op("ve", lambda EE: EE.memset(part, 0.0), inc="VE")

        scan_se, scan_ve, scan_mid, scan_te = [], [], [], []
        yev = []
        for k2p in range(0, NSC, 2):
            ld = []
            for k2 in (k2p, k2p + 1):
                sl = k2 % 2
                if k2 >= 2:
                    P.wait("sy", "VE", scan_mid[k2 - 2])
                D, j = divmod(k2, 16)
                ld.append(P.dma("sy", abf[:, 2 * sl * T:(2 * sl + 2) * T],
                                s_dtdu[D][j * 8:(j + 1) * 8].unsqueeze(1)
                                .broadcast_to([8, 16, 2 * T]), "SD%d" % sl))
            for q, k2 in enumerate((k2p, k2p + 1)):
                sl = k2 % 2
                dt_h = SL(2 * sl)
                P.wdma("se", ld[q])
                P.op("se", lambda EE, dt_h=dt_h, k2=k2:
                     EE.activation(dt_h, dt_h, AF.Exp, scale=c_acols[:, k2:k2 + 1]), inc="SE")
                scan_se.append(P.val("SE"))
            for q, k2 in enumerate((k2p, k2p + 1)):
                sl = k2 % 2
                D, j = divmod(k2, 16)
                dt_h = SL(2 * sl)
                du_h = SL(2 * sl + 1)
                b_in = SL(4)
                h_t = SL(5)
                hc_t = SL(6 + sl)
                P.wdma("ve", ld[q])
                if k2 == 0:
                    P.wdma("ve", v_ch)
                P.wait("ve", "SE", scan_se[k2])
                if k2 >= 2:
                    P.wait("ve", "TE", scan_te[k2 - 2])
                P.op("ve", lambda EE, b_in=b_in, du_h=du_h:
                     EE.tensor_tensor(b_in, du_h, bhh[:], ALU.mult))
                P.op("ve", lambda EE, h_t=h_t, dt_h=dt_h, b_in=b_in:
                     EE.tensor_tensor_scan(h_t, dt_h, b_in, 0.0, ALU.mult, ALU.add), inc="VE")
                scan_mid.append(P.val("VE"))
                pull_moe(1)
                P.op("ve", lambda EE, hc_t=hc_t, h_t=h_t:
                     EE.tensor_tensor(hc_t, h_t, chh[:], ALU.mult), inc="VE")
                scan_ve.append(P.val("VE"))
                P.wait("te", "VE", scan_ve[-1])
                if j == 0 and D >= 1:
                    P.wait("te", "VE", yev[D - 1][0])
                selv = c_sel[:, j * 128:(j + 1) * 128]
                for c in range(4):
                    P.op("te", lambda EE, selv=selv, hc_t=hc_t, c=c, j=j:
                         EE.matmul(psA[:, c * 512:(c + 1) * 512], selv,
                                   hc_t[:, c * 512:(c + 1) * 512],
                                   start=(j == 0), stop=(j == 15)),
                         inc=("TE" if c == 3 else None))
                scan_te.append(P.val("TE"))
                pull_moe(2)

                if j == 15:
                    szt = SL(8)
                    wt = SL(6)
                    y16 = SL(8)
                    tfp = SL(5)
                    if D >= 1:
                        P.wait("sy", "VE", yev[D - 1][0])
                        P.wdma("sy", yev[D - 1][1])
                    v_szl = P.dma("sy", szt, s_sz[D], "SZL")
                    P.wait("sy", "TE", scan_te[k2 - 1])
                    P.wait("sy", "VE", scan_ve[k2 - 1])
                    v_wl = P.dma("sy", wt, s_w[D], "SWL")
                    P.wait("ve", "TE", scan_te[-1])
                    P.wdma("ve", v_szl)
                    P.wdma("ve", v_wl)
                    if D >= 1:
                        P.wdma("ve", yev[D - 1][1])
                    P.op("ve", lambda EE, tfp=tfp, szt=szt:
                         EE.tensor_tensor(tfp, psA[:], szt, ALU.mult))
                    P.op("ve", lambda EE, y16=y16, tfp=tfp, wt=wt:
                         EE.tensor_tensor(y16, tfp, wt, ALU.add), inc="VE")
                    v_yev = P.val("VE")
                    yev.append((v_yev, None))
                    P.wait("sy", "VE", v_yev)
                    v_yst = P.dma("sy", s_y[D], y16, "YS")
                    yev[-1] = (v_yev, v_yst)
            pull_moe(2)

        v_p2_ve = P.val("VE")
        v_p2_se = P.val("SE")
        v_p2_te = P.val("TE")
        p2_toks = [P.tok(s) for s in ("SD0", "SD1", "SU0", "SU1", "SZL", "SWL", "YS")]

        wout_sb = abf[:, 1 * T:9 * T]
        pre = bhh[:].bitcast(F32)
        scv = chh[:].bitcast(F32)
        xv3 = c_sel[:].bitcast(F32)
        P.wait("gp", "VE", v_p2_ve)
        P.wait("gp", "SE", v_p2_se)
        P.wait("gp", "TE", v_p2_te)
        for tk in p2_toks:
            P.wdma("gp", tk)
        v_wout = P.dma("gp", wout_sb.rearrange("p (k h) -> p k h", k=DIT),
                       d_woutT.rearrange("k p h -> p k h"), "WO")

        pre_ve = []
        ep_se = []
        tile_ve = []
        ymm_te = []
        for i in range(TT):
            ybuf = SL(0)
            if i == 0:
                P.wait("sy", "SE", v_p2_se)
                P.wait("sy", "VE", v_p2_ve)
                P.wait("sy", "TE", v_p2_te)
            else:
                P.wait("sy", "TE", ymm_te[i - 1])
            if i == 0:
                P.wdma("sy", P.tok("YS"))
            yld = None
            for D in range(DIT):
                yld = P.dma("sy", ybuf[:, D * 128:(D + 1) * 128],
                            s_y[D][:, i * 128:(i + 1) * 128], "YL")
            if len(pre_ve) >= 2:
                P.wait("sy", "VE", pre_ve[-1])
            vx2 = P.dma("sy", xv3, d_x.rearrange("(i p) h -> i p h", p=128)[i], "XV")
            for hh in range(2):
                idx = i * 2 + hh
                pso = psA[:, (idx % 4) * 512:((idx % 4) + 1) * 512]
                P.wdma("te", yld)
                if i == 0 and hh == 0:
                    P.wdma("te", v_wout)
                    P.wait("te", "VE", v_p2_ve)
                if idx >= 4:
                    P.wait("te", "VE", pre_ve[idx - 4])
                for D in range(DIT):
                    P.op("te", lambda EE, pso=pso, ybuf=ybuf, D=D, hh=hh:
                         EE.matmul(pso, ybuf[:, D * 128:(D + 1) * 128],
                                   wout_sb[:, D * H + hh * 512: D * H + (hh + 1) * 512],
                                   start=(D == 0), stop=(D == DIT - 1)),
                         inc=("TE" if D == DIT - 1 else None))
                v_wo = P.val("TE")
                P.wait("ve", "TE", v_wo)
                P.wdma("ve", vx2)
                P.op("ve", lambda EE, pso=pso, hh=hh, i=i:
                     EE.scalar_tensor_tensor(pre[:, hh * 512:(hh + 1) * 512],
                                             xv3[:, hh * 512:(hh + 1) * 512],
                                             c_rstd[:, i * 8:i * 8 + 1], pso,
                                             ALU.mult, ALU.add), inc="VE")
                pre_ve.append(P.val("VE"))
            ymm_te.append(P.val("TE"))
            s_sum = sm8[:, 0:1]
            s_ssq = sm8[:, 1:2]
            mu8b = sm8[:, 8:16]
            q8b = sm8[:, 16:24]
            m28b = sm8[:, 24:32]
            ln8b = sm8[:, 32:40]
            rs8b = sm8[:, 40:48]
            v8b = sm8[:, 48:56]
            P.op("ve", lambda EE: EE.reduce_sum(s_sum, pre, axis=AX), inc="VE")
            P.wait("se", "VE", P.val("VE"))
            P.op("se", lambda EE: EE.activation(scv, pre, AF.Square, accum_out=s_ssq),
                 inc="SE")
            P.wait("ve", "SE", P.val("SE"))
            P.op("ve", lambda EE:
                 EE.tensor_scalar(mu8b, smE[:, 0:8], s_sum, 1.0 / H, ALU.mult, ALU.mult))
            P.op("ve", lambda EE:
                 EE.tensor_scalar(q8b, smE[:, 0:8], s_ssq, 1.0 / H, ALU.mult, ALU.mult), inc="VE")
            P.sop("ve", lambda EE: EE.tensor_tensor(m28b, mu8b, mu8b, ALU.mult))
            P.sop("ve", lambda EE: EE.tensor_tensor(v8b, q8b, m28b, ALU.subtract))
            P.wait("se", "VE", P.val("VE"))
            P.sop("se", lambda EE: EE.activation(ln8b, v8b, AF.Ln, bias=smE[:, 24:25]))
            P.sop("se", lambda EE: EE.activation(rs8b, ln8b, AF.Exp, scale=smE[:, 32:33]))
            P.wait("ve", "SE", P.val("SE"))
            P.op("ve", lambda EE:
                 EE.tensor_scalar(m28b, mu8b, rs8b[:, 0:1], -1.0, ALU.mult, ALU.mult), inc="VE")
            P.wait("se", "VE", P.val("VE"))
            P.op("se", lambda EE:
                 EE.activation(scv, pre, AF.Identity, scale=rs8b[:, 0:1], bias=m28b[:, 0:1]),
                 inc="SE")
            v_norm = P.val("SE")
            pv = part[:, i * H:(i + 1) * H]
            P.wait("ve", "SE", v_norm)
            if i == 0:
                P.wdma("ve", v_c2)
            P.op("ve", lambda EE: EE.tensor_tensor(pre, scv, c_lng[:], ALU.mult))
            P.op("ve", lambda EE, pv=pv:
                 EE.scalar_tensor_tensor(pv, pre, 1.0, pv, ALU.mult, ALU.add), inc="VE")
            tile_ve.append(P.val("VE"))
            pull_moe(10)

        pull_moe(3000)
        v_all_ve = P.val("VE")
        P.wait("gp", "TE", P.val("TE"))
        v_lnb5 = P.dma("gp", c_lnb, d_lnb, "C2")
        P.wait("sy", "VE", v_all_ve)
        P.wait("sy", "TE", P.val("TE"))
        st_ve = []
        for i in range(TT):
            x5 = abf[:, (i % 2) * T:(i % 2) * T + T].bitcast(F32)
            if i >= 2:
                P.wait("sy", "VE", st_ve[i - 2])
            vx5 = P.dma("sy", x5, d_x.rearrange("(i p) h -> i p h", p=128)[i], "X5%d" % (i % 2))
            pv = part[:, i * H:(i + 1) * H]
            P.wdma("ve", vx5)
            if i == 0:
                P.wdma("ve", v_lnb5)
            P.op("ve", lambda EE, pv=pv, x5=x5:
                 EE.scalar_tensor_tensor(pv, x5, 1.0, pv, ALU.mult, ALU.add))
            P.op("ve", lambda EE, pv=pv:
                 EE.scalar_tensor_tensor(pv, c_lnb[:], 1.0, pv, ALU.mult, ALU.add), inc="VE")
            st_ve.append(P.val("VE"))
            P.wait("sy", "VE", st_ve[-1])
            P.dma("sy", d_out.rearrange("(i p) h -> i p h", p=128)[i],
                  part[:, i * H:(i + 1) * H], "OS")
        P.wdma("sy", P.tok("OS"))

        @blk.sync
        def _(EE):
            for f in P.q["sy"]:
                f(EE)

        @blk.gpsimd
        def _(EE):
            for f in P.q["gp"]:
                f(EE)

        @blk.vector
        def _(EE):
            for f in P.q["ve"]:
                f(EE)

        @blk.scalar
        def _(EE):
            for f in P.q["se"]:
                f(EE)

        @blk.tensor
        def _(EE):
            for f in P.q["te"]:
                f(EE)

    return nc


def _host_layout(inputs):
    c = np.ascontiguousarray
    W_in = np.asarray(inputs["L0_attn_W_in"], np.float32)
    W_dt = np.asarray(inputs["L0_attn_W_dt"], np.float32)
    b_dt = np.asarray(inputs["L0_attn_b_dt"], np.float32)
    A_log = np.asarray(inputs["L0_attn_A_log"], np.float32)
    Dv = np.asarray(inputs["L0_attn_D"], np.float32)
    conv_w = np.asarray(inputs["L0_attn_conv_w"], np.float32)
    conv_b = np.asarray(inputs["L0_attn_conv_b"], np.float32)
    W_out = np.asarray(inputs["L0_attn_W_out"], np.float32)
    ln_g = np.asarray(inputs["L0_attn_ln_g"], np.float32)
    ln_b = np.asarray(inputs["L0_attn_ln_b"], np.float32)
    gate_w = np.asarray(inputs["L0_ffn_gate_w"], np.float32)
    Wgu = np.asarray(inputs["L0_ffn_gate_up_w"], np.float32)
    Wd = np.asarray(inputs["L0_ffn_down_w"], np.float32)

    d = {}
    W_in_p = np.zeros((33 * 128, H), np.float32)
    W_in_p[:4192] = W_in
    d["winT"] = c(W_in_p.reshape(33, 128, H).transpose(0, 2, 1))
    d["wdtT"] = c(W_dt.reshape(DIT, 128, RK).transpose(0, 2, 1))
    wgt = Wgu.reshape(E, 64, 128, H).transpose(0, 1, 3, 2)
    d["wguT"] = c(np.stack([wgt[:, :32], wgt[:, 32:]], axis=2))
    d["wdT"] = c(Wd.reshape(E, H, 32, 128).transpose(0, 2, 3, 1))
    d["woutT"] = c(W_out.reshape(H, 16, 128).transpose(1, 2, 0))
    d["gwT"] = c(gate_w.reshape(E, HT, 128).transpose(1, 2, 0))
    d["alogr"] = c(A_log.reshape(NSC, 8, NST).transpose(1, 2, 0).reshape(128, NSC))
    d["bdt"] = c(b_dt.reshape(DIT, 128).T)
    d["dcol"] = c(Dv.reshape(DIT, 128).T)
    d["cvw"] = c(conv_w.reshape(16, 128, 4).transpose(1, 0, 2).reshape(128, 64))
    d["cvb"] = c(conv_b.reshape(16, 128).T)
    d["lng"] = c(np.broadcast_to(ln_g, (128, H)))
    d["lnb"] = c(np.broadcast_to(ln_b, (128, H)))
    d["identf"] = np.eye(128, dtype=np.float32)
    sel = np.zeros((128, 16, 128), np.float32)
    p_idx = np.arange(128)
    for j in range(16):
        sel[p_idx, j, j * 8 + p_idx // 16] = 1.0
    d["sel"] = np.ascontiguousarray(sel.reshape(128, 2048)).astype(ml_dtypes.bfloat16)
    return d


_CACHE = {}


def kernel(**inputs):
    if "nc" not in _CACHE:
        _CACHE["nc"] = build_nc()
    nc = _CACHE["nc"]
    shared = _host_layout(inputs)
    x_full = np.ascontiguousarray(np.asarray(inputs["L0_attn_x"], np.float32))
    in_maps = []
    for core in range(8):
        m = dict(shared)
        m["x"] = np.ascontiguousarray(x_full[core])
        in_maps.append(m)
    res = run_bass_kernel_spmd(nc, in_maps, core_ids=list(range(8)))
    return np.stack([np.asarray(res.results[c]["out"]) for c in range(8)], axis=0)


# version 14 (final)
# speedup vs baseline: 1.2922x; 1.2922x over previous
import numpy as np
import ml_dtypes
from contextlib import ExitStack

import concourse.bass as bass
import concourse.mybir as mybir
from concourse.bass_utils import run_bass_kernel_spmd

F32 = mybir.dt.float32
BF16 = mybir.dt.bfloat16
U32 = mybir.dt.uint32
AF = mybir.ActivationFunctionType
ALU = mybir.AluOpType
AX = mybir.AxisListType.X

T = 2048
H = 1024
DI = 2048
NST = 16
RK = 64
E = 8
TT = 16
HT = 8
DIT = 16
NSC = 256
LN_EPS = 1e-5
G_PAIRS = 8
N_GROUPS = 4

ENGINES = ("sy", "gp", "ve", "se", "te")


class Prog:
    def __init__(self, nc, stack):
        self.nc = nc
        self._stack = stack
        self.q = {e: [] for e in ENGINES}
        self.sems = {}

    def sem(self, name, stack):
        h = stack.enter_context(self.nc.semaphore(name))
        self.sems[name] = [h, 0]

    def val(self, name):
        return self.sems[name][1]

    def wait(self, eng, sem, val):
        if val <= 0:
            return
        h = self.sems[sem][0]
        self.q[eng].append(lambda EE, h=h, v=val: EE.wait_ge(h, v))

    def op(self, eng, fn, inc=None, n=1):
        if inc is None:
            self.q[eng].append(fn)
            return None
        h = self.sems[inc][0]
        self.sems[inc][1] += n
        self.q[eng].append(lambda EE, fn=fn, h=h, n=n: fn(EE).then_inc(h, n))
        return self.sems[inc][1]

    def dma(self, eng, out_ap, in_ap, stream):
        if stream not in self.sems:
            h = self._stack.enter_context(self.nc.semaphore(stream))
            self.sems[stream] = [h, 0]
        self.op(eng, lambda EE: EE.dma_start(out_ap, in_ap), inc=stream, n=16)
        return (stream, self.sems[stream][1])

    def tok(self, stream):
        if stream not in self.sems:
            return None
        return (stream, self.sems[stream][1])

    def wdma(self, eng, tok):
        if tok is None:
            return
        stream, v = tok
        self.wait(eng, stream, v)

    def sop(self, eng, fn):
        sem = {"ve": "VE", "se": "SE"}[eng]
        self.wait(eng, sem, self.val(sem))
        return self.op(eng, fn, inc=sem)


def build_nc():
    nc = bass.Bass("TRN2", target_bir_lowering=False)

    d_x = nc.dram_tensor("x", [T, H], F32, kind="ExternalInput").ap()
    d_winT = nc.dram_tensor("winT", [33, H, 128], F32, kind="ExternalInput").ap()
    d_wdtT = nc.dram_tensor("wdtT", [DIT, RK, 128], F32, kind="ExternalInput").ap()
    d_wguT = nc.dram_tensor("wguT", [E, 32, 2, H, 128], F32, kind="ExternalInput").ap()
    d_wdT = nc.dram_tensor("wdT", [E, 32, 128, H], F32, kind="ExternalInput").ap()
    d_woutT = nc.dram_tensor("woutT", [DIT, 128, H], F32, kind="ExternalInput").ap()
    d_gwT = nc.dram_tensor("gwT", [HT, 128, E], F32, kind="ExternalInput").ap()
    d_alogr = nc.dram_tensor("alogr", [128, NSC], F32, kind="ExternalInput").ap()
    d_bdt = nc.dram_tensor("bdt", [128, DIT], F32, kind="ExternalInput").ap()
    d_dcol = nc.dram_tensor("dcol", [128, DIT], F32, kind="ExternalInput").ap()
    d_cvw = nc.dram_tensor("cvw", [128, 64], F32, kind="ExternalInput").ap()
    d_cvb = nc.dram_tensor("cvb", [128, DIT], F32, kind="ExternalInput").ap()
    d_lng = nc.dram_tensor("lng", [128, H], F32, kind="ExternalInput").ap()
    d_lnb = nc.dram_tensor("lnb", [128, H], F32, kind="ExternalInput").ap()
    d_identf = nc.dram_tensor("identf", [128, 128], F32, kind="ExternalInput").ap()
    d_sel = nc.dram_tensor("sel", [128, 16 * 128], BF16, kind="ExternalInput").ap()
    d_out = nc.dram_tensor("out", [T, H], F32, kind="ExternalOutput").ap()

    s_dtdu = nc.dram_tensor("s_dtdu", [DIT, 128, 2 * T], BF16, kind="Internal").ap()
    s_sz = nc.dram_tensor("s_sz", [DIT, 128, T], BF16, kind="Internal").ap()
    s_w = nc.dram_tensor("s_w", [DIT, 128, T], BF16, kind="Internal").ap()
    s_y = nc.dram_tensor("s_y", [DIT, 128, T], BF16, kind="Internal").ap()
    s_bc = nc.dram_tensor("s_bc", [32, T], F32, kind="Internal").ap()
    s_wgu16 = nc.dram_tensor("s_wgu16", [E, 32, 2, H, 128], BF16, kind="Internal").ap()
    s_wd16 = nc.dram_tensor("s_wd16", [E, 32, 128, H], BF16, kind="Internal").ap()

    with ExitStack() as ctx:
        P = Prog(nc, ctx)
        for s in ("VE", "SE", "TE", "GP"):
            P.sem(s, ctx)

        sbuf = lambda n, sh, dt=F32: ctx.enter_context(nc.sbuf_tensor(n, sh, dt))

        xnT16 = sbuf("xnT16", [128, HT * T], BF16)
        af32 = sbuf("af32", [128, 16 * 1024])
        abf = sbuf("abf", [128, 9 * T], BF16)
        warena = sbuf("warena", [128, 8 * T], BF16)
        marena = sbuf("marena", [128, 4 * 1024], BF16)
        wdb = sbuf("wdb", [128, 8 * 1024], BF16)
        gelb16 = sbuf("gelb16", [128, 1024], BF16)
        c_lng = sbuf("c_lng", [128, H], BF16)
        c_lnb = marena[:, 0:1024]
        c_sel = sbuf("c_sel", [128, 16 * 128], BF16)
        c_acols = sbuf("c_acols", [128, NSC])
        c_rstd = sbuf("c_rstd", [128, TT * 8])
        c_cmb16 = sbuf("c_cmb16", [128, TT * E], BF16)
        sm8 = sbuf("sm8", [128, 8 * 8])
        c_identf = gelb16[:, 0:256].bitcast(F32)
        stage = gelb16[:, 256:512].bitcast(F32)
        c_gw = wdb[:, 0:128].bitcast(F32)
        c_bdt = wdb[:, 128:160].bitcast(F32)
        c_dcol = wdb[:, 160:192].bitcast(F32)
        c_cvw = wdb[:, 192:320].bitcast(F32)
        c_cvb = wdb[:, 320:352].bitcast(F32)
        smE = sbuf("smE", [128, 5 * 8])
        bhh = sbuf("bhh", [128, T], BF16)
        chh = sbuf("chh", [128, T], BF16)

        psA = ctx.enter_context(nc.psum_tensor("psA", [128, T], F32))
        psB = ctx.enter_context(nc.psum_tensor("psB", [128, T], F32))

        SL = lambda s: abf[:, s * T:(s + 1) * T]

        blk = ctx.enter_context(nc.Block())

        P.dma("sy", c_identf, d_identf, "C")
        P.dma("sy", c_sel[:], d_sel, "C")
        P.dma("sy", af32[:, 8192:8448], d_alogr, "C")
        P.dma("sy", c_bdt[:], d_bdt, "C")
        P.dma("sy", c_dcol[:], d_dcol, "C")
        P.dma("sy", c_cvw[:], d_cvw, "C")
        P.dma("sy", c_cvb[:], d_cvb, "C")

        v_consts = P.dma("sy", c_gw.rearrange("p (j e) -> p j e", j=HT),
                         d_gwT.rearrange("j p e -> p j e"), "C")
        v_c2 = P.dma("gp", c_lng[:], d_lng, "C2")
        P.wdma("se", v_consts)
        P.op("se", lambda EE: EE.activation(af32[:, 8192:8448], af32[:, 8192:8448], AF.Exp), inc="SE")
        v_acols_se = P.val("SE")
        P.wait("ve", "SE", v_acols_se)
        P.op("ve", lambda EE: EE.tensor_scalar_mul(c_acols[:], af32[:, 8192:8448], -1.0))
        P.op("ve", lambda EE: EE.memset(smE[:, 0:8], 1.0))
        P.op("ve", lambda EE: EE.memset(smE[:, 8:16], -1e30))
        P.op("ve", lambda EE: EE.memset(smE[:, 16:24], 0.0))
        P.op("ve", lambda EE: EE.memset(smE[:, 24:32], LN_EPS))
        P.op("ve", lambda EE: EE.memset(smE[:, 32:40], -0.5), inc="VE")
        v_acols = P.val("VE")

        cv_tok = [None] * E

        def conv_track():
            for e in range(E):
                for p0 in range(0, 32, 8):
                    P.dma("gp", s_wgu16[e, p0:p0 + 8], d_wguT[e, p0:p0 + 8], "CV")
                for f0 in range(0, 32, 16):
                    cv_tok[e] = P.dma("gp", s_wd16[e, f0:f0 + 16], d_wdT[e, f0:f0 + 16], "CV")
                yield

        conv_gen = conv_track()

        def pull_conv(n):
            for _ in range(n):
                next(conv_gen, None)

        pull_conv(2)

        ev_vals = []
        rmm_vals = []
        x_rel = []
        for i in range(TT):
            slot = i % 2
            xv = af32[:, slot * H:(slot + 1) * H]
            if i >= 2:
                P.wait("sy", "TE", x_rel[i - 2])
            vx = P.dma("sy", xv, d_x.rearrange("(i p) h -> i p h", p=128)[i], "X%d" % slot)

            base = 4096 + slot * 64
            sm1 = af32[:, base + 0:base + 1]
            ssqv = af32[:, base + 1:base + 2]
            mu8 = af32[:, base + 8:base + 16]
            q8 = af32[:, base + 16:base + 24]
            m28 = af32[:, base + 24:base + 32]
            lnv8 = af32[:, base + 32:base + 40]
            P.wdma("ve", vx)
            P.op("ve", lambda EE, sm1=sm1, xv=xv: EE.reduce_sum(sm1, xv, axis=AX), inc="VE")
            v_mu = P.val("VE")
            sq = af32[:, 4224:4224 + H]
            P.wdma("se", vx)
            P.wait("se", "VE", v_mu)
            P.op("se", lambda EE, sq=sq, xv=xv, ssqv=ssqv:
                 EE.activation(sq, xv, AF.Square, accum_out=ssqv), inc="SE")
            v_ssq = P.val("SE")
            P.wait("ve", "SE", v_ssq)
            P.op("ve", lambda EE, mu8=mu8, sm1=sm1:
                 EE.tensor_scalar(mu8, smE[:, 0:8], sm1, 1.0 / H, ALU.mult, ALU.mult))
            P.op("ve", lambda EE, q8=q8, ssqv=ssqv:
                 EE.tensor_scalar(q8, smE[:, 0:8], ssqv, 1.0 / H, ALU.mult, ALU.mult), inc="VE")
            P.sop("ve", lambda EE, m28=m28, mu8=mu8: EE.tensor_tensor(m28, mu8, mu8, ALU.mult))
            var8 = af32[:, base + 40:base + 48]
            P.sop("ve", lambda EE, var8=var8, q8=q8, m28=m28:
                 EE.tensor_tensor(var8, q8, m28, ALU.subtract))
            v_var = P.val("VE")
            rstd8 = c_rstd[:, i * 8:(i + 1) * 8]
            nmr8 = af32[:, base + 48:base + 56]
            P.wait("se", "VE", v_var)
            P.sop("se", lambda EE, lnv8=lnv8, var8=var8:
                 EE.activation(lnv8, var8, AF.Ln, bias=smE[:, 24:25]))
            P.sop("se", lambda EE, rstd8=rstd8, lnv8=lnv8:
                 EE.activation(rstd8, lnv8, AF.Exp, scale=smE[:, 32:33]))
            v_rstd = P.val("SE")
            P.wait("ve", "SE", v_rstd)
            P.op("ve", lambda EE, nmr8=nmr8, mu8=mu8, rstd8=rstd8:
                 EE.tensor_scalar(nmr8, mu8, rstd8[:, 0:1], -1.0, ALU.mult, ALU.mult), inc="VE")
            v_nmr = P.val("VE")

            xnv = af32[:, 2048 + slot * H: 2048 + (slot + 1) * H]
            P.wait("se", "VE", v_nmr)
            if i >= 2:
                P.wait("se", "TE", x_rel[i - 2])
            P.op("se", lambda EE, xnv=xnv, xv=xv, rstd8=rstd8, nmr8=nmr8:
                 EE.activation(xnv, xv, AF.Identity, scale=rstd8[:, 0:1], bias=nmr8[:, 0:1]), inc="SE")
            v_xn = P.val("SE")

            P.wait("te", "SE", v_xn)
            if i == 0:
                P.wdma("te", v_consts)
                P.wait("te", "VE", v_acols)
            for j in range(HT):
                idx = i * HT + j
                pst = psA[:, (j % 2) * 512:(j % 2) * 512 + 128]
                if idx >= 2:
                    P.wait("te", "VE", ev_vals[idx - 2])
                P.op("te", lambda EE, pst=pst, xnv=xnv, j=j:
                     EE.transpose(pst, xnv[:, j * 128:(j + 1) * 128], c_identf), inc="TE")
                v_tr = P.val("TE")
                stg = stage[:, 0:128]
                P.wait("ve", "TE", v_tr)
                if idx >= 1:
                    P.wait("ve", "TE", rmm_vals[idx - 1])
                P.op("ve", lambda EE, stg=stg, pst=pst: EE.tensor_copy(stg, pst))
                P.op("ve", lambda EE, pst=pst, j=j, i=i:
                     EE.tensor_copy(xnT16[:, j * T + i * 128: j * T + (i + 1) * 128], pst),
                     inc="VE")
                ev_vals.append(P.val("VE"))
                P.wait("te", "VE", ev_vals[-1])
                P.op("te", lambda EE, stg=stg, j=j:
                     EE.matmul(psB[:, 0:8], stg, c_gw[:, j * E:(j + 1) * E],
                               start=(j == 0), stop=(j == HT - 1)), inc="TE")
                rmm_vals.append(P.val("TE"))
            x_rel.append(P.val("TE"))
            v_rmm = P.val("TE")

            lg = sm8[:, 0:8]; mx = sm8[:, 8:16]; nx8 = sm8[:, 16:24]; ex = sm8[:, 24:32]
            m1 = sm8[:, 32:40]; mk = sm8[:, 40:48]; m2 = sm8[:, 48:56]
            e1 = sm8[:, 56:64]
            e2 = sm8[:, 16:24]
            ws = sm8[:, 8:16]
            rw = sm8[:, 8:16]
            w1 = sm8[:, 0:8]
            w2 = sm8[:, 24:32]
            tmp8b = sm8[:, 48:56]
            cmbv = c_cmb16[:, i * E:(i + 1) * E]
            P.wait("ve", "TE", v_rmm)
            P.sop("ve", lambda EE, lg=lg: EE.tensor_copy(lg, psB[:, 0:8]))
            P.sop("ve", lambda EE, mx=mx, lg=lg: EE.reduce_max(mx[:, 0:1], lg, axis=AX))
            P.sop("ve", lambda EE, nx8=nx8, mx=mx:
                  EE.tensor_scalar(nx8, smE[:, 0:8], mx[:, 0:1], -1.0, ALU.mult, ALU.mult))
            P.wait("se", "VE", P.val("VE"))
            P.sop("se", lambda EE, ex=ex, lg=lg, nx8=nx8:
                  EE.activation(ex, lg, AF.Exp, bias=nx8[:, 0:1]))
            P.wait("ve", "SE", P.val("SE"))
            P.sop("ve", lambda EE, m1=m1, ex=ex: EE.reduce_max(m1[:, 0:1], ex, axis=AX))
            P.sop("ve", lambda EE, e1=e1, ex=ex, m1=m1:
                  EE.tensor_single_scalar(e1, ex, m1[:, 0:1], ALU.is_ge))
            P.sop("ve", lambda EE, mk=mk, e1=e1, ex=ex:
                  EE.select(mk, e1.bitcast(U32), smE[:, 8:16], ex))
            P.sop("ve", lambda EE, m2=m2, mk=mk: EE.reduce_max(m2[:, 0:1], mk, axis=AX))
            P.sop("ve", lambda EE, e2=e2, mk=mk, m2=m2:
                  EE.tensor_single_scalar(e2, mk, m2[:, 0:1], ALU.is_ge))
            P.sop("ve", lambda EE, ws=ws, m1=m1:
                  EE.tensor_scalar(ws, smE[:, 0:8], m1[:, 0:1], None, ALU.mult))
            P.sop("ve", lambda EE, mk=mk, ws=ws, m2=m2:
                  EE.tensor_single_scalar(mk, ws, m2[:, 0:1], ALU.add))
            P.sop("ve", lambda EE, rw=rw, mk=mk: EE.reciprocal(rw, mk))
            P.sop("ve", lambda EE, w1=w1, rw=rw, m1=m1:
                  EE.tensor_single_scalar(w1, rw, m1[:, 0:1], ALU.mult))
            P.sop("ve", lambda EE, w2=w2, rw=rw, m2=m2:
                  EE.tensor_single_scalar(w2, rw, m2[:, 0:1], ALU.mult))
            P.sop("ve", lambda EE, tmp8b=tmp8b, e2=e2, w2=w2:
                  EE.select(tmp8b, e2.bitcast(U32), w2, smE[:, 16:24]))
            P.sop("ve", lambda EE, cmbv=cmbv, e1=e1, w1=w1, tmp8b=tmp8b:
                  EE.select(cmbv, e1.bitcast(U32), w1, tmp8b))
            if i % 3 == 2:
                pull_conv(1)

        win_ring = [[0, 0] for _ in range(4)]

        def plan_win_load(ot, slot):
            P.wait("gp", "TE", win_ring[slot][1])
            ov = marena[:, slot * 1024:(slot + 1) * 1024]
            v = P.dma("gp", ov.rearrange("p (j m) -> p j m", j=HT),
                      d_winT[ot].rearrange("(j p) m -> p j m", p=128), "WIN%d" % slot)
            win_ring[slot][0] = v
            return ov

        w32 = plan_win_load(32, 0)
        P.wdma("te", win_ring[0][0])
        P.wait("te", "VE", ev_vals[-1])
        for c in range(4):
            for j in range(HT):
                P.op("te", lambda EE, c=c, j=j, w32=w32:
                     EE.matmul(psA[0:32, c * 512:(c + 1) * 512],
                               w32[:, j * 128:j * 128 + 32],
                               xnT16[:, j * T + c * 512: j * T + (c + 1) * 512],
                               start=(j == 0), stop=(j == HT - 1)),
                     inc=("TE" if (c == 3 and j == HT - 1) else None))
        v_bcmm = P.val("TE")
        for c in range(4):
            for j in range(HT):
                P.op("te", lambda EE, c=c, j=j, w32=w32:
                     EE.matmul(psB[0:64, c * 512:(c + 1) * 512],
                               w32[:, j * 128 + 32:j * 128 + 96],
                               xnT16[:, j * T + c * 512: j * T + (c + 1) * 512],
                               start=(j == 0), stop=(j == HT - 1)),
                     inc=("TE" if (c == 3 and j == HT - 1) else None))
        v_o32 = P.val("TE")
        win_ring[0][1] = v_o32

        bc_sb = af32[0:32, 0:T]
        dtp16 = abf[0:64, 0:T]
        P.wait("ve", "TE", v_o32)
        P.op("ve", lambda EE: EE.tensor_copy(bc_sb, psA[0:32, :]), inc="VE")
        v_bc = P.val("VE")
        P.op("ve", lambda EE: EE.tensor_copy(dtp16, psB[0:64, :]), inc="VE")
        v_dtp = P.val("VE")
        P.wait("sy", "VE", v_bc)
        v_bc_st = P.dma("sy", s_bc, bc_sb, "BCS")
        P.wdma("gp", v_bc_st)
        v_bh = P.dma("gp", bhh[:], s_bc[0:16].unsqueeze(0).broadcast_to([8, 16, T]), "BH")
        v_ch = P.dma("gp", chh[:], s_bc[16:32].unsqueeze(0).broadcast_to([8, 16, T]), "BH")

        wdt_sb = warena[0:64, 0:T]
        v_wdt = P.dma("gp", wdt_sb.rearrange("r (k m) -> r k m", k=DIT),
                      d_wdtT.rearrange("k r m -> r k m"), "WDT")
        dt_se, dt_ve, dt_spill = [], [], []
        for k in range(DIT):
            pdt = psA if k % 2 == 0 else psB
            if k == 0:
                P.wdma("te", v_wdt)
                P.wait("te", "VE", v_bc)
            if k == 1:
                P.wait("te", "VE", v_dtp)
            if k >= 2:
                P.wait("te", "SE", dt_se[k - 2])
            for c in range(4):
                P.op("te", lambda EE, pdt=pdt, k=k, c=c:
                     EE.matmul(pdt[:, c * 512:(c + 1) * 512],
                               wdt_sb[:, k * 128:(k + 1) * 128],
                               dtp16[:, c * 512:(c + 1) * 512],
                               start=True, stop=True),
                     inc=("TE" if c == 3 else None))
            v_mm = P.val("TE")
            rbuf = af32[:, 10368:12416]
            abuf = af32[:, 4224:6272]
            bcol = c_bdt[:, k:k + 1]
            P.wait("se", "TE", v_mm)
            if k >= 1:
                P.wait("se", "VE", dt_ve[k - 1])
            P.op("se", lambda EE, rbuf=rbuf, pdt=pdt, bcol=bcol:
                 EE.activation(rbuf, pdt[:], AF.Relu, bias=bcol))
            P.op("se", lambda EE, abuf=abuf, pdt=pdt, bcol=bcol:
                 EE.activation(abuf, pdt[:], AF.Abs, bias=bcol), inc="SE")
            dt_se.append(P.val("SE"))
            P.op("se", lambda EE, abuf=abuf: EE.activation(abuf, abuf, AF.Exp, scale=-1.0))
            P.op("se", lambda EE, abuf=abuf: EE.activation(abuf, abuf, AF.Ln, bias=1.0), inc="SE")
            v_spl = P.val("SE")
            dt16 = SL(1 + k % 2)
            P.wait("ve", "SE", v_spl)
            if k >= 2:
                P.wdma("ve", dt_spill[k - 2])
            P.op("ve", lambda EE, dt16=dt16, rbuf=rbuf, abuf=abuf:
                 EE.tensor_tensor(dt16, rbuf, abuf, ALU.add), inc="VE")
            dt_ve.append(P.val("VE"))
            P.wait("sy", "VE", dt_ve[-1])
            dt_spill.append(P.dma("sy", s_dtdu[k][:, 0:T], dt16, "DS%d" % (k % 2)))
        v_dt_se_all = P.val("SE")

        pair_conv, pair_silu, pair_ve, pair_spill = [], [], [], []
        for k in range(DIT):
            sx = (2 * k) % 4
            sz_ = (2 * k + 1) % 4
            wx = plan_win_load(k, sx)
            wz = plan_win_load(16 + k, sz_)
            P.wdma("te", win_ring[sx][0])
            P.wdma("te", win_ring[sz_][0])
            if k == 0:
                P.wait("te", "SE", v_dt_se_all)
            else:
                P.wait("te", "VE", pair_conv[k - 1])
                P.wait("te", "SE", pair_silu[k - 1])
            for c in range(4):
                for j in range(HT):
                    P.op("te", lambda EE, wx=wx, j=j, c=c:
                         EE.matmul(psA[:, c * 512:(c + 1) * 512],
                                   wx[:, j * 128:(j + 1) * 128],
                                   xnT16[:, j * T + c * 512: j * T + (c + 1) * 512],
                                   start=(j == 0), stop=(j == HT - 1)),
                         inc=("TE" if (c == 3 and j == HT - 1) else None))
            v_px = P.val("TE")
            for c in range(4):
                for j in range(HT):
                    P.op("te", lambda EE, wz=wz, j=j, c=c:
                         EE.matmul(psB[:, c * 512:(c + 1) * 512],
                                   wz[:, j * 128:(j + 1) * 128],
                                   xnT16[:, j * T + c * 512: j * T + (c + 1) * 512],
                                   start=(j == 0), stop=(j == HT - 1)),
                         inc=("TE" if (c == 3 and j == HT - 1) else None))
            v_pair = P.val("TE")
            win_ring[sx][1] = v_pair
            win_ring[sz_][1] = v_pair

            xb = af32[:, 2048:2048 + T + 3]
            t1 = af32[:, 6272:8320]
            t2 = af32[:, 8320:10368]
            u32 = af32[:, 2048:4096]
            sz32 = af32[:, 4224:6272]
            P.wait("ve", "TE", v_px)
            if k == 0:
                P.wait("ve", "SE", v_dt_se_all)
            P.op("ve", lambda EE, xb=xb: EE.memset(xb[:, 0:3], 0.0))
            P.op("ve", lambda EE, xb=xb: EE.tensor_copy(xb[:, 3:3 + T], psA[:]))
            P.op("ve", lambda EE, t1=t1, xb=xb, k=k:
                 EE.tensor_single_scalar(t1, xb[:, 0:T], c_cvw[:, 4 * k:4 * k + 1], ALU.mult))
            P.op("ve", lambda EE, t2=t2, xb=xb, t1=t1, k=k:
                 EE.scalar_tensor_tensor(t2, xb[:, 1:1 + T], c_cvw[:, 4 * k + 1:4 * k + 2],
                                         t1, ALU.mult, ALU.add))
            P.op("ve", lambda EE, t1=t1, xb=xb, t2=t2, k=k:
                 EE.scalar_tensor_tensor(t1, xb[:, 2:2 + T], c_cvw[:, 4 * k + 2:4 * k + 3],
                                         t2, ALU.mult, ALU.add))
            P.op("ve", lambda EE, t2=t2, xb=xb, t1=t1, k=k:
                 EE.scalar_tensor_tensor(t2, xb[:, 3:3 + T], c_cvw[:, 4 * k + 3:4 * k + 4],
                                         t1, ALU.mult, ALU.add), inc="VE")
            pair_conv.append(P.val("VE"))
            P.wait("se", "VE", pair_conv[-1])
            P.wait("se", "TE", v_pair)
            P.op("se", lambda EE, u32=u32, t2=t2, k=k:
                 EE.activation(u32, t2, AF.Silu, bias=c_cvb[:, k:k + 1]), inc="SE")
            P.op("se", lambda EE, sz32=sz32: EE.activation(sz32, psB[:], AF.Silu), inc="SE")
            pair_silu.append(P.val("SE"))

            dt16r = SL(3 + k % 2)
            P.wdma("sy", dt_spill[k])
            if k >= 2:
                P.wait("sy", "VE", pair_ve[k - 2])
            v_dtr = P.dma("sy", dt16r, s_dtdu[k][:, 0:T], "DR%d" % (k % 2))
            du16 = SL(5 + k % 2)
            sz16 = SL(7)
            w16 = SL(8)
            P.wait("ve", "SE", pair_silu[-1])
            P.wdma("ve", v_dtr)
            if k >= 1:
                P.wdma("ve", pair_spill[k - 1][0])
                P.wdma("ve", pair_spill[k - 1][1])
            P.op("ve", lambda EE, du16=du16, dt16r=dt16r, u32=u32:
                 EE.tensor_tensor(du16, dt16r, u32, ALU.mult))
            P.op("ve", lambda EE, sz16=sz16, sz32=sz32: EE.tensor_copy(sz16, sz32))
            P.op("ve", lambda EE, t1=t1, u32=u32, k=k:
                 EE.tensor_single_scalar(t1, u32, c_dcol[:, k:k + 1], ALU.mult))
            P.op("ve", lambda EE, w16=w16, t1=t1, sz32=sz32:
                 EE.tensor_tensor(w16, t1, sz32, ALU.mult), inc="VE")
            pair_ve.append(P.val("VE"))
            P.wait("sy", "VE", pair_ve[-1])
            P.dma("sy", s_dtdu[k][:, T:2 * T], du16, "US%d" % (k % 2))
            v_szst = P.dma("sy", s_sz[k], sz16, "SS")
            v_wst = P.dma("sy", s_w[k], w16, "WS%d" % (k % 2))
            pair_spill.append((v_szst, v_wst))

        v_p1_ve = P.val("VE")
        v_p1_se = P.val("SE")
        v_p1_te = P.val("TE")
        p1_spill_toks = [P.tok(s) for s in
                         ("DS0", "DS1", "US0", "US1", "SS", "WS0", "WS1", "DR0", "DR1")]

        pull_conv(8)

        part = af32[:, 0:TT * H]
        v_p1_ve = P.val("VE")
        v_p1_se = P.val("SE")
        v_p1_te = P.val("TE")

        moe_state = {
            "wgu_rel": [v_p1_te] * 2,
            "wd_rel": v_p1_te,
            "wd_tok": None,
            "hact_rel": [v_p1_te] * G_PAIRS,
            "brel": [0, 0, 0, 0],
            "gel_rel": [0, 0],
            "first": True,
        }

        def moe_track():
            st = moe_state
            wslab = 0
            for e in range(E):
                for g in range(N_GROUPS):
                    if g == 0:
                        P.wdma("gp", cv_tok[e])
                    if e == 0 and g == 0:
                        P.wait("gp", "VE", v_p1_ve)
                    for fj in range(G_PAIRS):
                        pr = g * G_PAIRS + fj
                        ps_ = wslab % 2
                        wslab += 1
                        P.wait("gp", "TE", st["wgu_rel"][ps_])
                        pslot = marena[:, ps_ * 2048:(ps_ + 1) * 2048]
                        vg = P.dma("gp", pslot.rearrange("p (u j m) -> p u j m", u=2, j=HT),
                                   s_wgu16[e, pr].rearrange("u (j p) m -> p u j m", p=128),
                                   "GU%d" % ps_)
                        vu = vg
                        wg = pslot[:, 0:1024]
                        wu = pslot[:, 1024:2048]
                        if fj == 1:
                            P.wait("gp", "TE", st["wd_rel"])
                            for q4 in range(2):
                                st["wd_tok"] = P.dma(
                                    "gp", wdb[:, q4 * 4096:(q4 + 1) * 4096].rearrange(
                                        "p (f h) -> p f h", f=4),
                                    s_wd16[e, g * G_PAIRS + q4 * 4:g * G_PAIRS + (q4 + 1) * 4]
                                    .rearrange("f p h -> p f h"), "WDL")
                        hrow = warena[:, fj * T:(fj + 1) * T]
                        for c in range(4):
                            pc = c % 2
                            psg = psB[:, pc * 512:(pc + 1) * 512]
                            psu = psB[:, (2 + pc) * 512:(3 + pc) * 512]
                            P.wdma("te", vg)
                            P.wdma("te", vu)
                            if st["first"]:
                                P.wait("te", "SE", v_p1_se)
                                P.wait("te", "VE", v_p1_ve)
                                st["first"] = False
                            P.wait("te", "VE", st["brel"][pc])
                            P.wait("te", "VE", st["brel"][2 + pc])
                            for j in range(HT):
                                P.op("te", lambda EE, psg=psg, wg=wg, j=j, c=c:
                                     EE.matmul(psg, wg[:, j * 128:(j + 1) * 128],
                                               xnT16[:, j * T + c * 512: j * T + (c + 1) * 512],
                                               start=(j == 0), stop=(j == HT - 1)),
                                     inc=("TE" if j == HT - 1 else None))
                            v_g = P.val("TE")
                            for j in range(HT):
                                P.op("te", lambda EE, psu=psu, wu=wu, j=j, c=c:
                                     EE.matmul(psu, wu[:, j * 128:(j + 1) * 128],
                                               xnT16[:, j * T + c * 512: j * T + (c + 1) * 512],
                                               start=(j == 0), stop=(j == HT - 1)),
                                     inc=("TE" if j == HT - 1 else None))
                            v_u = P.val("TE")
                            gelb = gelb16[:, pc * 512:(pc + 1) * 512]
                            P.wait("se", "TE", v_g)
                            P.wait("se", "VE", st["gel_rel"][pc])
                            P.op("se", lambda EE, gelb=gelb, psg=psg:
                                 EE.activation(gelb, psg, AF.Gelu), inc="SE")
                            v_gel = P.val("SE")
                            P.wait("ve", "SE", v_gel)
                            P.wait("ve", "TE", v_u)
                            if c == 0:
                                P.wait("ve", "TE", st["hact_rel"][fj])
                            P.op("ve", lambda EE, hrow=hrow, psu=psu, gelb=gelb, c=c:
                                 EE.tensor_tensor(hrow[:, c * 512:(c + 1) * 512], psu, gelb,
                                                  ALU.mult), inc="VE")
                            st["brel"][pc] = P.val("VE")
                            st["brel"][2 + pc] = P.val("VE")
                            st["gel_rel"][pc] = P.val("VE")
                            yield
                        st["wgu_rel"][ps_] = P.val("TE")
                    v_hact_all = P.val("VE")

                    P.wdma("te", st["wd_tok"])
                    P.wait("te", "VE", v_hact_all)
                    for i in range(TT):
                        for hh in range(2):
                            idx = i * 2 + hh
                            psd = psB[:, (idx % 2) * 512:(idx % 2 + 1) * 512]
                            P.wait("te", "VE", st["brel"][idx % 2])
                            for fj in range(G_PAIRS):
                                P.op("te", lambda EE, psd=psd, fj=fj, i=i, hh=hh:
                                     EE.matmul(psd,
                                               warena[:, fj * T + i * 128: fj * T + (i + 1) * 128],
                                               wdb[:, fj * 1024 + hh * 512: fj * 1024 + (hh + 1) * 512],
                                               start=(fj == 0), stop=(fj == G_PAIRS - 1)),
                                     inc=("TE" if fj == G_PAIRS - 1 else None))
                            v_dn = P.val("TE")
                            pv = part[:, i * H + hh * 512: i * H + (hh + 1) * 512]
                            P.wait("ve", "TE", v_dn)
                            P.op("ve", lambda EE, pv=pv, psd=psd, i=i, e=e:
                                 EE.scalar_tensor_tensor(pv, psd[:], c_cmb16[:, i * E + e:i * E + e + 1],
                                                         pv, ALU.mult, ALU.add), inc="VE")
                            st["brel"][idx % 2] = P.val("VE")
                            if hh == 1:
                                yield
                    v_grp_te = P.val("TE")
                    moe_state["wd_rel"] = v_grp_te
                    for fj in range(G_PAIRS):
                        moe_state["hact_rel"][fj] = v_grp_te
            while True:
                yield None

        moe_gen = moe_track()

        def pull_moe(n):
            for _ in range(n):
                next(moe_gen)

        p1_spill_toks = [P.tok(s) for s in
                         ("DS0", "DS1", "US0", "US1", "SS", "WS0", "WS1", "DR0", "DR1")]
        for tk in p1_spill_toks:
            P.wdma("sy", tk)
        P.wait("sy", "TE", v_p1_te)
        P.wait("sy", "VE", v_p1_ve)
        P.wait("sy", "SE", v_p1_se)

        P.wait("ve", "SE", v_p1_se)
        P.op("ve", lambda EE: EE.memset(part, 0.0), inc="VE")

        scan_se = [None] * NSC
        scan_mid = [None] * NSC
        scan_ve = [None] * NSC
        scan_te = [None] * NSC
        ld = [None, None]
        yev = []

        def issue_sd(k2q):
            sq = k2q % 2
            if k2q >= 2:
                P.wait("sy", "VE", scan_mid[k2q - 2])
            Dq, jq = divmod(k2q, 16)
            ld[sq] = P.dma("sy", abf[:, 2 * sq * T:(2 * sq + 2) * T],
                           s_dtdu[Dq][jq * 8:(jq + 1) * 8].unsqueeze(1)
                           .broadcast_to([8, 16, 2 * T]), "SD%d" % sq)

        def emit_sel(k2):
            D, j = divmod(k2, 16)
            hc_t = SL(6 + k2 % 2)
            P.wait("te", "VE", scan_ve[k2])
            if j == 0 and D >= 1:
                P.wait("te", "VE", yev[D - 1][0])
            selv = c_sel[:, j * 128:(j + 1) * 128]
            for c in range(4):
                P.op("te", lambda EE, selv=selv, hc_t=hc_t, c=c, j=j:
                     EE.matmul(psA[:, c * 512:(c + 1) * 512], selv,
                               hc_t[:, c * 512:(c + 1) * 512],
                               start=(j == 0), stop=(j == 15)),
                     inc=("TE" if c == 3 else None))
            scan_te[k2] = P.val("TE")

        def emit_epi(D):
            k2l = D * 16 + 15
            szt = SL(8)
            wt = SL(4)
            y16 = SL(8)
            tfp = SL(5)
            if D >= 1:
                P.wait("sy", "VE", yev[D - 1][0])
                P.wdma("sy", yev[D - 1][1])
            v_szl = P.dma("sy", szt, s_sz[D], "SZL")
            k2cur = (D + 1) * 16 if D < 15 else NSC - 1
            P.wait("sy", "VE", scan_mid[k2cur])
            v_wl = P.dma("sy", wt, s_w[D], "SWL")
            P.wait("ve", "TE", scan_te[k2l])
            P.wdma("ve", v_szl)
            P.wdma("ve", v_wl)
            if D >= 1:
                P.wdma("ve", yev[D - 1][1])
            P.op("ve", lambda EE, tfp=tfp, szt=szt:
                 EE.tensor_tensor(tfp, psA[:], szt, ALU.mult))
            P.op("ve", lambda EE, y16=y16, tfp=tfp, wt=wt:
                 EE.tensor_tensor(y16, tfp, wt, ALU.add), inc="VE")
            v_yev = P.val("VE")
            yev.append((v_yev, None))
            P.wait("sy", "VE", v_yev)
            v_yst = P.dma("sy", s_y[D], y16, "YS")
            yev[-1] = (v_yev, v_yst)

        issue_sd(0)
        issue_sd(1)
        for k2 in range(NSC):
            sl = k2 % 2
            D, j = divmod(k2, 16)
            dt_h = SL(2 * sl)
            du_h = SL(2 * sl + 1)
            b_in = SL(4)
            h_t = SL(5)
            hc_t = SL(6 + sl)
            if k2 >= 1:
                emit_sel(k2 - 1)
            P.wdma("se", ld[sl])
            P.op("se", lambda EE, dt_h=dt_h, k2=k2:
                 EE.activation(dt_h, dt_h, AF.Exp, scale=c_acols[:, k2:k2 + 1]), inc="SE")
            scan_se[k2] = P.val("SE")
            pull_moe(1)
            P.wdma("ve", ld[sl])
            if k2 == 0:
                P.wdma("ve", v_ch)
            P.wait("ve", "SE", scan_se[k2])
            P.op("ve", lambda EE, b_in=b_in, du_h=du_h:
                 EE.tensor_tensor(b_in, du_h, bhh[:], ALU.mult))
            pull_moe(1)
            P.op("ve", lambda EE, h_t=h_t, dt_h=dt_h, b_in=b_in:
                 EE.tensor_tensor_scan(h_t, dt_h, b_in, 0.0, ALU.mult, ALU.add), inc="VE")
            scan_mid[k2] = P.val("VE")
            if sl == 1 and k2 + 1 < NSC:
                issue_sd(k2 + 1)
                if k2 + 2 < NSC:
                    issue_sd(k2 + 2)
            pull_moe(1)
            if k2 >= 2:
                P.wait("ve", "TE", scan_te[k2 - 2])
            P.op("ve", lambda EE, hc_t=hc_t, h_t=h_t:
                 EE.tensor_tensor(hc_t, h_t, chh[:], ALU.mult), inc="VE")
            scan_ve[k2] = P.val("VE")
            pull_moe(1)
            if j == 0 and D >= 1:
                emit_epi(D - 1)
        emit_sel(NSC - 1)
        emit_epi(15)

        v_p2_ve = P.val("VE")
        v_p2_se = P.val("SE")
        v_p2_te = P.val("TE")
        p2_toks = [P.tok(s) for s in ("SD0", "SD1", "SU0", "SU1", "SZL", "SWL", "YS")]

        wout_sb = abf[:, 1 * T:9 * T]
        pre = bhh[:].bitcast(F32)
        scv = chh[:].bitcast(F32)
        xv3 = c_sel[:].bitcast(F32)
        P.wait("gp", "VE", v_p2_ve)
        P.wait("gp", "SE", v_p2_se)
        P.wait("gp", "TE", v_p2_te)
        for tk in p2_toks:
            P.wdma("gp", tk)
        v_wout = P.dma("gp", wout_sb.rearrange("p (k h) -> p k h", k=DIT),
                       d_woutT.rearrange("k p h -> p k h"), "WO")

        pre_ve = []
        ep_se = []
        tile_ve = []
        ymm_te = []
        for i in range(TT):
            ybuf = SL(0)
            if i == 0:
                P.wait("sy", "SE", v_p2_se)
                P.wait("sy", "VE", v_p2_ve)
                P.wait("sy", "TE", v_p2_te)
            else:
                P.wait("sy", "TE", ymm_te[i - 1])
            if i == 0:
                P.wdma("sy", P.tok("YS"))
            yld = None
            for D in range(DIT):
                yld = P.dma("sy", ybuf[:, D * 128:(D + 1) * 128],
                            s_y[D][:, i * 128:(i + 1) * 128], "YL")
            if len(pre_ve) >= 2:
                P.wait("sy", "VE", pre_ve[-1])
            vx2 = P.dma("sy", xv3, d_x.rearrange("(i p) h -> i p h", p=128)[i], "XV")
            for hh in range(2):
                idx = i * 2 + hh
                pso = psA[:, (idx % 4) * 512:((idx % 4) + 1) * 512]
                P.wdma("te", yld)
                if i == 0 and hh == 0:
                    P.wdma("te", v_wout)
                    P.wait("te", "VE", v_p2_ve)
                if idx >= 4:
                    P.wait("te", "VE", pre_ve[idx - 4])
                for D in range(DIT):
                    P.op("te", lambda EE, pso=pso, ybuf=ybuf, D=D, hh=hh:
                         EE.matmul(pso, ybuf[:, D * 128:(D + 1) * 128],
                                   wout_sb[:, D * H + hh * 512: D * H + (hh + 1) * 512],
                                   start=(D == 0), stop=(D == DIT - 1)),
                         inc=("TE" if D == DIT - 1 else None))
                v_wo = P.val("TE")
                P.wait("ve", "TE", v_wo)
                P.wdma("ve", vx2)
                P.op("ve", lambda EE, pso=pso, hh=hh, i=i:
                     EE.scalar_tensor_tensor(pre[:, hh * 512:(hh + 1) * 512],
                                             xv3[:, hh * 512:(hh + 1) * 512],
                                             c_rstd[:, i * 8:i * 8 + 1], pso,
                                             ALU.mult, ALU.add), inc="VE")
                pre_ve.append(P.val("VE"))
            ymm_te.append(P.val("TE"))
            s_sum = sm8[:, 0:1]
            s_ssq = sm8[:, 1:2]
            mu8b = sm8[:, 8:16]
            q8b = sm8[:, 16:24]
            m28b = sm8[:, 24:32]
            ln8b = sm8[:, 32:40]
            rs8b = sm8[:, 40:48]
            v8b = sm8[:, 48:56]
            P.op("ve", lambda EE: EE.reduce_sum(s_sum, pre, axis=AX), inc="VE")
            P.wait("se", "VE", P.val("VE"))
            P.op("se", lambda EE: EE.activation(scv, pre, AF.Square, accum_out=s_ssq),
                 inc="SE")
            P.wait("ve", "SE", P.val("SE"))
            P.op("ve", lambda EE:
                 EE.tensor_scalar(mu8b, smE[:, 0:8], s_sum, 1.0 / H, ALU.mult, ALU.mult))
            P.op("ve", lambda EE:
                 EE.tensor_scalar(q8b, smE[:, 0:8], s_ssq, 1.0 / H, ALU.mult, ALU.mult), inc="VE")
            P.sop("ve", lambda EE: EE.tensor_tensor(m28b, mu8b, mu8b, ALU.mult))
            P.sop("ve", lambda EE: EE.tensor_tensor(v8b, q8b, m28b, ALU.subtract))
            P.wait("se", "VE", P.val("VE"))
            P.sop("se", lambda EE: EE.activation(ln8b, v8b, AF.Ln, bias=smE[:, 24:25]))
            P.sop("se", lambda EE: EE.activation(rs8b, ln8b, AF.Exp, scale=smE[:, 32:33]))
            P.wait("ve", "SE", P.val("SE"))
            P.op("ve", lambda EE:
                 EE.tensor_scalar(m28b, mu8b, rs8b[:, 0:1], -1.0, ALU.mult, ALU.mult), inc="VE")
            P.wait("se", "VE", P.val("VE"))
            P.op("se", lambda EE:
                 EE.activation(scv, pre, AF.Identity, scale=rs8b[:, 0:1], bias=m28b[:, 0:1]),
                 inc="SE")
            v_norm = P.val("SE")
            pv = part[:, i * H:(i + 1) * H]
            P.wait("ve", "SE", v_norm)
            if i == 0:
                P.wdma("ve", v_c2)
            P.op("ve", lambda EE: EE.tensor_tensor(pre, scv, c_lng[:], ALU.mult))
            P.op("ve", lambda EE, pv=pv:
                 EE.scalar_tensor_tensor(pv, pre, 1.0, pv, ALU.mult, ALU.add), inc="VE")
            tile_ve.append(P.val("VE"))
            pull_moe(10)

        pull_moe(3000)
        v_all_ve = P.val("VE")
        P.wait("gp", "TE", P.val("TE"))
        v_lnb5 = P.dma("gp", c_lnb, d_lnb, "C2")
        P.wait("sy", "VE", v_all_ve)
        P.wait("sy", "TE", P.val("TE"))
        st_ve = []
        for i in range(TT):
            x5 = abf[:, (i % 2) * T:(i % 2) * T + T].bitcast(F32)
            if i >= 2:
                P.wait("sy", "VE", st_ve[i - 2])
            vx5 = P.dma("sy", x5, d_x.rearrange("(i p) h -> i p h", p=128)[i], "X5%d" % (i % 2))
            pv = part[:, i * H:(i + 1) * H]
            P.wdma("ve", vx5)
            if i == 0:
                P.wdma("ve", v_lnb5)
            P.op("ve", lambda EE, pv=pv, x5=x5:
                 EE.scalar_tensor_tensor(pv, x5, 1.0, pv, ALU.mult, ALU.add))
            P.op("ve", lambda EE, pv=pv:
                 EE.scalar_tensor_tensor(pv, c_lnb[:], 1.0, pv, ALU.mult, ALU.add), inc="VE")
            st_ve.append(P.val("VE"))
            P.wait("sy", "VE", st_ve[-1])
            P.dma("sy", d_out.rearrange("(i p) h -> i p h", p=128)[i],
                  part[:, i * H:(i + 1) * H], "OS")
        P.wdma("sy", P.tok("OS"))

        @blk.sync
        def _(EE):
            for f in P.q["sy"]:
                f(EE)

        @blk.gpsimd
        def _(EE):
            for f in P.q["gp"]:
                f(EE)

        @blk.vector
        def _(EE):
            for f in P.q["ve"]:
                f(EE)

        @blk.scalar
        def _(EE):
            for f in P.q["se"]:
                f(EE)

        @blk.tensor
        def _(EE):
            for f in P.q["te"]:
                f(EE)

    return nc


def _host_layout(inputs):
    c = np.ascontiguousarray
    W_in = np.asarray(inputs["L0_attn_W_in"], np.float32)
    W_dt = np.asarray(inputs["L0_attn_W_dt"], np.float32)
    b_dt = np.asarray(inputs["L0_attn_b_dt"], np.float32)
    A_log = np.asarray(inputs["L0_attn_A_log"], np.float32)
    Dv = np.asarray(inputs["L0_attn_D"], np.float32)
    conv_w = np.asarray(inputs["L0_attn_conv_w"], np.float32)
    conv_b = np.asarray(inputs["L0_attn_conv_b"], np.float32)
    W_out = np.asarray(inputs["L0_attn_W_out"], np.float32)
    ln_g = np.asarray(inputs["L0_attn_ln_g"], np.float32)
    ln_b = np.asarray(inputs["L0_attn_ln_b"], np.float32)
    gate_w = np.asarray(inputs["L0_ffn_gate_w"], np.float32)
    Wgu = np.asarray(inputs["L0_ffn_gate_up_w"], np.float32)
    Wd = np.asarray(inputs["L0_ffn_down_w"], np.float32)

    d = {}
    W_in_p = np.zeros((33 * 128, H), np.float32)
    W_in_p[:4192] = W_in
    d["winT"] = c(W_in_p.reshape(33, 128, H).transpose(0, 2, 1))
    d["wdtT"] = c(W_dt.reshape(DIT, 128, RK).transpose(0, 2, 1))
    wgt = Wgu.reshape(E, 64, 128, H).transpose(0, 1, 3, 2)
    d["wguT"] = c(np.stack([wgt[:, :32], wgt[:, 32:]], axis=2))
    d["wdT"] = c(Wd.reshape(E, H, 32, 128).transpose(0, 2, 3, 1))
    d["woutT"] = c(W_out.reshape(H, 16, 128).transpose(1, 2, 0))
    d["gwT"] = c(gate_w.reshape(E, HT, 128).transpose(1, 2, 0))
    d["alogr"] = c(A_log.reshape(NSC, 8, NST).transpose(1, 2, 0).reshape(128, NSC))
    d["bdt"] = c(b_dt.reshape(DIT, 128).T)
    d["dcol"] = c(Dv.reshape(DIT, 128).T)
    d["cvw"] = c(conv_w.reshape(16, 128, 4).transpose(1, 0, 2).reshape(128, 64))
    d["cvb"] = c(conv_b.reshape(16, 128).T)
    d["lng"] = c(np.broadcast_to(ln_g, (128, H)))
    d["lnb"] = c(np.broadcast_to(ln_b, (128, H)))
    d["identf"] = np.eye(128, dtype=np.float32)
    sel = np.zeros((128, 16, 128), np.float32)
    p_idx = np.arange(128)
    for j in range(16):
        sel[p_idx, j, j * 8 + p_idx // 16] = 1.0
    d["sel"] = np.ascontiguousarray(sel.reshape(128, 2048)).astype(ml_dtypes.bfloat16)
    return d


_CACHE = {}


def kernel(**inputs):
    if "nc" not in _CACHE:
        _CACHE["nc"] = build_nc()
    nc = _CACHE["nc"]
    shared = _host_layout(inputs)
    x_full = np.ascontiguousarray(np.asarray(inputs["L0_attn_x"], np.float32))
    in_maps = []
    for core in range(8):
        m = dict(shared)
        m["x"] = np.ascontiguousarray(x_full[core])
        in_maps.append(m)
    res = run_bass_kernel_spmd(nc, in_maps, core_ids=list(range(8)))
    return np.stack([np.asarray(res.results[c]["out"]) for c in range(8)], axis=0)


# version 15
# speedup vs baseline: 1.4401x; 1.1144x over previous
import numpy as np
import ml_dtypes
from contextlib import ExitStack

import concourse.bass as bass
import concourse.mybir as mybir
from concourse.bass_utils import run_bass_kernel_spmd

F32 = mybir.dt.float32
BF16 = mybir.dt.bfloat16
U32 = mybir.dt.uint32
AF = mybir.ActivationFunctionType
ALU = mybir.AluOpType
AX = mybir.AxisListType.X

T = 2048
H = 1024
DI = 2048
NST = 16
RK = 64
E = 8
TT = 16
HT = 8
DIT = 16
NSC = 256
LN_EPS = 1e-5
G_PAIRS = 8
N_GROUPS = 4

ENGINES = ("sy", "gp", "ve", "se", "te")


class Prog:
    def __init__(self, nc, stack):
        self.nc = nc
        self._stack = stack
        self.q = {e: [] for e in ENGINES}
        self.sems = {}

    def sem(self, name, stack):
        h = stack.enter_context(self.nc.semaphore(name))
        self.sems[name] = [h, 0]

    def val(self, name):
        return self.sems[name][1]

    def wait(self, eng, sem, val):
        if val <= 0:
            return
        h = self.sems[sem][0]
        self.q[eng].append(lambda EE, h=h, v=val: EE.wait_ge(h, v))

    def op(self, eng, fn, inc=None, n=1):
        if inc is None:
            self.q[eng].append(fn)
            return None
        h = self.sems[inc][0]
        self.sems[inc][1] += n
        self.q[eng].append(lambda EE, fn=fn, h=h, n=n: fn(EE).then_inc(h, n))
        return self.sems[inc][1]

    def dma(self, eng, out_ap, in_ap, stream):
        if stream not in self.sems:
            h = self._stack.enter_context(self.nc.semaphore(stream))
            self.sems[stream] = [h, 0]
        self.op(eng, lambda EE: EE.dma_start(out_ap, in_ap), inc=stream, n=16)
        return (stream, self.sems[stream][1])

    def tok(self, stream):
        if stream not in self.sems:
            return None
        return (stream, self.sems[stream][1])

    def wdma(self, eng, tok):
        if tok is None:
            return
        stream, v = tok
        self.wait(eng, stream, v)

    def sop(self, eng, fn):
        sem = {"ve": "VE", "se": "SE"}[eng]
        self.wait(eng, sem, self.val(sem))
        return self.op(eng, fn, inc=sem)


def build_nc():
    nc = bass.Bass("TRN2", target_bir_lowering=False)

    d_x = nc.dram_tensor("x", [T, H], F32, kind="ExternalInput").ap()
    d_winT = nc.dram_tensor("winT", [33, H, 128], F32, kind="ExternalInput").ap()
    d_wdtT = nc.dram_tensor("wdtT", [DIT, RK, 128], F32, kind="ExternalInput").ap()
    d_wguT = nc.dram_tensor("wguT", [E, 32, 2, H, 128], F32, kind="ExternalInput").ap()
    d_wdT = nc.dram_tensor("wdT", [E, 32, 128, H], F32, kind="ExternalInput").ap()
    d_woutT = nc.dram_tensor("woutT", [DIT, 128, H], F32, kind="ExternalInput").ap()
    d_gwT = nc.dram_tensor("gwT", [HT, 128, E], F32, kind="ExternalInput").ap()
    d_alogr = nc.dram_tensor("alogr", [128, NSC], F32, kind="ExternalInput").ap()
    d_bdt = nc.dram_tensor("bdt", [128, DIT], F32, kind="ExternalInput").ap()
    d_dcol = nc.dram_tensor("dcol", [128, DIT], F32, kind="ExternalInput").ap()
    d_cvw = nc.dram_tensor("cvw", [128, 64], F32, kind="ExternalInput").ap()
    d_cvb = nc.dram_tensor("cvb", [128, DIT], F32, kind="ExternalInput").ap()
    d_lng = nc.dram_tensor("lng", [128, H], F32, kind="ExternalInput").ap()
    d_lnb = nc.dram_tensor("lnb", [128, H], F32, kind="ExternalInput").ap()
    d_identf = nc.dram_tensor("identf", [128, 128], F32, kind="ExternalInput").ap()
    d_sel = nc.dram_tensor("sel", [128, 16 * 128], BF16, kind="ExternalInput").ap()
    d_out = nc.dram_tensor("out", [T, H], F32, kind="ExternalOutput").ap()

    s_dtdu = nc.dram_tensor("s_dtdu", [DIT, 128, 2 * T], BF16, kind="Internal").ap()
    s_sz = nc.dram_tensor("s_sz", [DIT, 128, T], BF16, kind="Internal").ap()
    s_w = nc.dram_tensor("s_w", [DIT, 128, T], BF16, kind="Internal").ap()
    s_y = nc.dram_tensor("s_y", [DIT, 128, T], BF16, kind="Internal").ap()
    s_bc = nc.dram_tensor("s_bc", [32, T], F32, kind="Internal").ap()

    with ExitStack() as ctx:
        P = Prog(nc, ctx)
        for s in ("VE", "SE", "TE", "GP"):
            P.sem(s, ctx)

        sbuf = lambda n, sh, dt=F32: ctx.enter_context(nc.sbuf_tensor(n, sh, dt))

        xnT16 = sbuf("xnT16", [128, HT * T], BF16)
        af32 = sbuf("af32", [128, 16 * 1024])
        abf = sbuf("abf", [128, 9 * T], BF16)
        warena = sbuf("warena", [128, 8 * T], BF16)
        marena = sbuf("marena", [128, 4 * 1024], BF16)
        wdb = sbuf("wdb", [128, 8 * 1024], BF16)
        gelb16 = sbuf("gelb16", [128, 1024], BF16)
        c_lng = sbuf("c_lng", [128, H], BF16)
        c_lnb = marena[:, 0:1024]
        c_sel = sbuf("c_sel", [128, 16 * 128], BF16)
        c_acols = sbuf("c_acols", [128, NSC])
        c_rstd = sbuf("c_rstd", [128, TT * 8])
        c_cmb16 = sbuf("c_cmb16", [128, TT * E], BF16)
        sm8 = sbuf("sm8", [128, 8 * 8])
        c_identf = gelb16[:, 0:256].bitcast(F32)
        stage = gelb16[:, 256:512].bitcast(F32)
        c_gw = wdb[:, 0:128].bitcast(F32)
        c_bdt = wdb[:, 128:160].bitcast(F32)
        c_dcol = wdb[:, 160:192].bitcast(F32)
        c_cvw = wdb[:, 192:320].bitcast(F32)
        c_cvb = wdb[:, 320:352].bitcast(F32)
        smE = sbuf("smE", [128, 5 * 8])
        bhh = sbuf("bhh", [128, T], BF16)
        chh = sbuf("chh", [128, T], BF16)

        psA = ctx.enter_context(nc.psum_tensor("psA", [128, T], F32))
        psB = ctx.enter_context(nc.psum_tensor("psB", [128, T], F32))

        SL = lambda s: abf[:, s * T:(s + 1) * T]

        blk = ctx.enter_context(nc.Block())

        P.dma("sy", c_identf, d_identf, "C")
        P.dma("sy", c_sel[:], d_sel, "C")
        P.dma("sy", af32[:, 8192:8448], d_alogr, "C")
        P.dma("sy", c_bdt[:], d_bdt, "C")
        P.dma("sy", c_dcol[:], d_dcol, "C")
        P.dma("sy", c_cvw[:], d_cvw, "C")
        P.dma("sy", c_cvb[:], d_cvb, "C")

        v_consts = P.dma("sy", c_gw.rearrange("p (j e) -> p j e", j=HT),
                         d_gwT.rearrange("j p e -> p j e"), "C")
        v_c2 = P.dma("gp", c_lng[:], d_lng, "C2")
        P.wdma("se", v_consts)
        P.op("se", lambda EE: EE.activation(af32[:, 8192:8448], af32[:, 8192:8448], AF.Exp), inc="SE")
        v_acols_se = P.val("SE")
        P.wait("ve", "SE", v_acols_se)
        P.op("ve", lambda EE: EE.tensor_scalar_mul(c_acols[:], af32[:, 8192:8448], -1.0))
        P.op("ve", lambda EE: EE.memset(smE[:, 0:8], 1.0))
        P.op("ve", lambda EE: EE.memset(smE[:, 8:16], -1e30))
        P.op("ve", lambda EE: EE.memset(smE[:, 16:24], 0.0))
        P.op("ve", lambda EE: EE.memset(smE[:, 24:32], LN_EPS))
        P.op("ve", lambda EE: EE.memset(smE[:, 32:40], -0.5), inc="VE")
        v_acols = P.val("VE")

        ev_vals = []
        rmm_vals = []
        x_rel = []
        for i in range(TT):
            slot = i % 2
            xv = af32[:, slot * H:(slot + 1) * H]
            if i >= 2:
                P.wait("sy", "TE", x_rel[i - 2])
            vx = P.dma("sy", xv, d_x.rearrange("(i p) h -> i p h", p=128)[i], "X%d" % slot)

            base = 4096 + slot * 64
            sm1 = af32[:, base + 0:base + 1]
            ssqv = af32[:, base + 1:base + 2]
            mu8 = af32[:, base + 8:base + 16]
            q8 = af32[:, base + 16:base + 24]
            m28 = af32[:, base + 24:base + 32]
            lnv8 = af32[:, base + 32:base + 40]
            P.wdma("ve", vx)
            P.op("ve", lambda EE, sm1=sm1, xv=xv: EE.reduce_sum(sm1, xv, axis=AX), inc="VE")
            v_mu = P.val("VE")
            sq = af32[:, 4224:4224 + H]
            P.wdma("se", vx)
            P.wait("se", "VE", v_mu)
            P.op("se", lambda EE, sq=sq, xv=xv, ssqv=ssqv:
                 EE.activation(sq, xv, AF.Square, accum_out=ssqv), inc="SE")
            v_ssq = P.val("SE")
            P.wait("ve", "SE", v_ssq)
            P.op("ve", lambda EE, mu8=mu8, sm1=sm1:
                 EE.tensor_scalar(mu8, smE[:, 0:8], sm1, 1.0 / H, ALU.mult, ALU.mult))
            P.op("ve", lambda EE, q8=q8, ssqv=ssqv:
                 EE.tensor_scalar(q8, smE[:, 0:8], ssqv, 1.0 / H, ALU.mult, ALU.mult), inc="VE")
            P.sop("ve", lambda EE, m28=m28, mu8=mu8: EE.tensor_tensor(m28, mu8, mu8, ALU.mult))
            var8 = af32[:, base + 40:base + 48]
            P.sop("ve", lambda EE, var8=var8, q8=q8, m28=m28:
                 EE.tensor_tensor(var8, q8, m28, ALU.subtract))
            v_var = P.val("VE")
            rstd8 = c_rstd[:, i * 8:(i + 1) * 8]
            nmr8 = af32[:, base + 48:base + 56]
            P.wait("se", "VE", v_var)
            P.sop("se", lambda EE, lnv8=lnv8, var8=var8:
                 EE.activation(lnv8, var8, AF.Ln, bias=smE[:, 24:25]))
            P.sop("se", lambda EE, rstd8=rstd8, lnv8=lnv8:
                 EE.activation(rstd8, lnv8, AF.Exp, scale=smE[:, 32:33]))
            v_rstd = P.val("SE")
            P.wait("ve", "SE", v_rstd)
            P.op("ve", lambda EE, nmr8=nmr8, mu8=mu8, rstd8=rstd8:
                 EE.tensor_scalar(nmr8, mu8, rstd8[:, 0:1], -1.0, ALU.mult, ALU.mult), inc="VE")
            v_nmr = P.val("VE")

            xnv = af32[:, 2048 + slot * H: 2048 + (slot + 1) * H]
            P.wait("se", "VE", v_nmr)
            if i >= 2:
                P.wait("se", "TE", x_rel[i - 2])
            P.op("se", lambda EE, xnv=xnv, xv=xv, rstd8=rstd8, nmr8=nmr8:
                 EE.activation(xnv, xv, AF.Identity, scale=rstd8[:, 0:1], bias=nmr8[:, 0:1]), inc="SE")
            v_xn = P.val("SE")

            P.wait("te", "SE", v_xn)
            if i == 0:
                P.wdma("te", v_consts)
                P.wait("te", "VE", v_acols)
            for j in range(HT):
                idx = i * HT + j
                pst = psA[:, (j % 2) * 512:(j % 2) * 512 + 128]
                if idx >= 2:
                    P.wait("te", "VE", ev_vals[idx - 2])
                P.op("te", lambda EE, pst=pst, xnv=xnv, j=j:
                     EE.transpose(pst, xnv[:, j * 128:(j + 1) * 128], c_identf), inc="TE")
                v_tr = P.val("TE")
                stg = stage[:, 0:128]
                P.wait("ve", "TE", v_tr)
                if idx >= 1:
                    P.wait("ve", "TE", rmm_vals[idx - 1])
                P.op("ve", lambda EE, stg=stg, pst=pst: EE.tensor_copy(stg, pst))
                P.op("ve", lambda EE, pst=pst, j=j, i=i:
                     EE.tensor_copy(xnT16[:, j * T + i * 128: j * T + (i + 1) * 128], pst),
                     inc="VE")
                ev_vals.append(P.val("VE"))
                P.wait("te", "VE", ev_vals[-1])
                P.op("te", lambda EE, stg=stg, j=j:
                     EE.matmul(psB[:, 0:8], stg, c_gw[:, j * E:(j + 1) * E],
                               start=(j == 0), stop=(j == HT - 1)), inc="TE")
                rmm_vals.append(P.val("TE"))
            x_rel.append(P.val("TE"))
            v_rmm = P.val("TE")

            lg = sm8[:, 0:8]; mx = sm8[:, 8:16]; nx8 = sm8[:, 16:24]; ex = sm8[:, 24:32]
            m1 = sm8[:, 32:40]; mk = sm8[:, 40:48]; m2 = sm8[:, 48:56]
            e1 = sm8[:, 56:64]
            e2 = sm8[:, 16:24]
            ws = sm8[:, 8:16]
            rw = sm8[:, 8:16]
            w1 = sm8[:, 0:8]
            w2 = sm8[:, 24:32]
            tmp8b = sm8[:, 48:56]
            cmbv = c_cmb16[:, i * E:(i + 1) * E]
            P.wait("ve", "TE", v_rmm)
            P.sop("ve", lambda EE, lg=lg: EE.tensor_copy(lg, psB[:, 0:8]))
            P.sop("ve", lambda EE, mx=mx, lg=lg: EE.reduce_max(mx[:, 0:1], lg, axis=AX))
            P.sop("ve", lambda EE, nx8=nx8, mx=mx:
                  EE.tensor_scalar(nx8, smE[:, 0:8], mx[:, 0:1], -1.0, ALU.mult, ALU.mult))
            P.wait("se", "VE", P.val("VE"))
            P.sop("se", lambda EE, ex=ex, lg=lg, nx8=nx8:
                  EE.activation(ex, lg, AF.Exp, bias=nx8[:, 0:1]))
            P.wait("ve", "SE", P.val("SE"))
            P.sop("ve", lambda EE, m1=m1, ex=ex: EE.reduce_max(m1[:, 0:1], ex, axis=AX))
            P.sop("ve", lambda EE, e1=e1, ex=ex, m1=m1:
                  EE.tensor_single_scalar(e1, ex, m1[:, 0:1], ALU.is_ge))
            P.sop("ve", lambda EE, mk=mk, e1=e1, ex=ex:
                  EE.select(mk, e1.bitcast(U32), smE[:, 8:16], ex))
            P.sop("ve", lambda EE, m2=m2, mk=mk: EE.reduce_max(m2[:, 0:1], mk, axis=AX))
            P.sop("ve", lambda EE, e2=e2, mk=mk, m2=m2:
                  EE.tensor_single_scalar(e2, mk, m2[:, 0:1], ALU.is_ge))
            P.sop("ve", lambda EE, ws=ws, m1=m1:
                  EE.tensor_scalar(ws, smE[:, 0:8], m1[:, 0:1], None, ALU.mult))
            P.sop("ve", lambda EE, mk=mk, ws=ws, m2=m2:
                  EE.tensor_single_scalar(mk, ws, m2[:, 0:1], ALU.add))
            P.sop("ve", lambda EE, rw=rw, mk=mk: EE.reciprocal(rw, mk))
            P.sop("ve", lambda EE, w1=w1, rw=rw, m1=m1:
                  EE.tensor_single_scalar(w1, rw, m1[:, 0:1], ALU.mult))
            P.sop("ve", lambda EE, w2=w2, rw=rw, m2=m2:
                  EE.tensor_single_scalar(w2, rw, m2[:, 0:1], ALU.mult))
            P.sop("ve", lambda EE, tmp8b=tmp8b, e2=e2, w2=w2:
                  EE.select(tmp8b, e2.bitcast(U32), w2, smE[:, 16:24]))
            P.sop("ve", lambda EE, cmbv=cmbv, e1=e1, w1=w1, tmp8b=tmp8b:
                  EE.select(cmbv, e1.bitcast(U32), w1, tmp8b))

        win_ring = [[0, 0] for _ in range(4)]

        def plan_win_load(ot, slot):
            P.wait("gp", "TE", win_ring[slot][1])
            ov = marena[:, slot * 1024:(slot + 1) * 1024]
            v = P.dma("gp", ov.rearrange("p (j m) -> p j m", j=HT),
                      d_winT[ot].rearrange("(j p) m -> p j m", p=128), "WIN%d" % slot)
            win_ring[slot][0] = v
            return ov

        w32 = plan_win_load(32, 0)
        P.wdma("te", win_ring[0][0])
        P.wait("te", "VE", ev_vals[-1])
        for c in range(4):
            for j in range(HT):
                P.op("te", lambda EE, c=c, j=j, w32=w32:
                     EE.matmul(psA[0:32, c * 512:(c + 1) * 512],
                               w32[:, j * 128:j * 128 + 32],
                               xnT16[:, j * T + c * 512: j * T + (c + 1) * 512],
                               start=(j == 0), stop=(j == HT - 1)),
                     inc=("TE" if (c == 3 and j == HT - 1) else None))
        v_bcmm = P.val("TE")
        for c in range(4):
            for j in range(HT):
                P.op("te", lambda EE, c=c, j=j, w32=w32:
                     EE.matmul(psB[0:64, c * 512:(c + 1) * 512],
                               w32[:, j * 128 + 32:j * 128 + 96],
                               xnT16[:, j * T + c * 512: j * T + (c + 1) * 512],
                               start=(j == 0), stop=(j == HT - 1)),
                     inc=("TE" if (c == 3 and j == HT - 1) else None))
        v_o32 = P.val("TE")
        win_ring[0][1] = v_o32

        bc_sb = af32[0:32, 0:T]
        dtp16 = abf[0:64, 0:T]
        P.wait("ve", "TE", v_o32)
        P.op("ve", lambda EE: EE.tensor_copy(bc_sb, psA[0:32, :]), inc="VE")
        v_bc = P.val("VE")
        P.op("ve", lambda EE: EE.tensor_copy(dtp16, psB[0:64, :]), inc="VE")
        v_dtp = P.val("VE")
        P.wait("sy", "VE", v_bc)
        v_bc_st = P.dma("sy", s_bc, bc_sb, "BCS")
        P.wdma("gp", v_bc_st)
        v_bh = P.dma("gp", bhh[:], s_bc[0:16].unsqueeze(0).broadcast_to([8, 16, T]), "BH")
        v_ch = P.dma("gp", chh[:], s_bc[16:32].unsqueeze(0).broadcast_to([8, 16, T]), "BH")

        wdt_sb = warena[0:64, 0:T]
        v_wdt = P.dma("gp", wdt_sb.rearrange("r (k m) -> r k m", k=DIT),
                      d_wdtT.rearrange("k r m -> r k m"), "WDT")
        dt_se, dt_ve, dt_spill = [], [], []
        for k in range(DIT):
            pdt = psA if k % 2 == 0 else psB
            if k == 0:
                P.wdma("te", v_wdt)
                P.wait("te", "VE", v_bc)
            if k == 1:
                P.wait("te", "VE", v_dtp)
            if k >= 2:
                P.wait("te", "SE", dt_se[k - 2])
            for c in range(4):
                P.op("te", lambda EE, pdt=pdt, k=k, c=c:
                     EE.matmul(pdt[:, c * 512:(c + 1) * 512],
                               wdt_sb[:, k * 128:(k + 1) * 128],
                               dtp16[:, c * 512:(c + 1) * 512],
                               start=True, stop=True),
                     inc=("TE" if c == 3 else None))
            v_mm = P.val("TE")
            rbuf = af32[:, 10368:12416]
            abuf = af32[:, 4224:6272]
            bcol = c_bdt[:, k:k + 1]
            P.wait("se", "TE", v_mm)
            if k >= 1:
                P.wait("se", "VE", dt_ve[k - 1])
            P.op("se", lambda EE, rbuf=rbuf, pdt=pdt, bcol=bcol:
                 EE.activation(rbuf, pdt[:], AF.Relu, bias=bcol))
            P.op("se", lambda EE, abuf=abuf, pdt=pdt, bcol=bcol:
                 EE.activation(abuf, pdt[:], AF.Abs, bias=bcol), inc="SE")
            dt_se.append(P.val("SE"))
            P.op("se", lambda EE, abuf=abuf: EE.activation(abuf, abuf, AF.Exp, scale=-1.0))
            P.op("se", lambda EE, abuf=abuf: EE.activation(abuf, abuf, AF.Ln, bias=1.0), inc="SE")
            v_spl = P.val("SE")
            dt16 = SL(1 + k % 2)
            P.wait("ve", "SE", v_spl)
            if k >= 2:
                P.wdma("ve", dt_spill[k - 2])
            P.op("ve", lambda EE, dt16=dt16, rbuf=rbuf, abuf=abuf:
                 EE.tensor_tensor(dt16, rbuf, abuf, ALU.add), inc="VE")
            dt_ve.append(P.val("VE"))
            P.wait("sy", "VE", dt_ve[-1])
            dt_spill.append(P.dma("sy", s_dtdu[k][:, 0:T], dt16, "DS%d" % (k % 2)))
        v_dt_se_all = P.val("SE")

        pair_conv, pair_silu, pair_ve, pair_spill = [], [], [], []
        for k in range(DIT):
            sx = (2 * k) % 4
            sz_ = (2 * k + 1) % 4
            wx = plan_win_load(k, sx)
            wz = plan_win_load(16 + k, sz_)
            P.wdma("te", win_ring[sx][0])
            P.wdma("te", win_ring[sz_][0])
            if k == 0:
                P.wait("te", "SE", v_dt_se_all)
            else:
                P.wait("te", "VE", pair_conv[k - 1])
                P.wait("te", "SE", pair_silu[k - 1])
            for c in range(4):
                for j in range(HT):
                    P.op("te", lambda EE, wx=wx, j=j, c=c:
                         EE.matmul(psA[:, c * 512:(c + 1) * 512],
                                   wx[:, j * 128:(j + 1) * 128],
                                   xnT16[:, j * T + c * 512: j * T + (c + 1) * 512],
                                   start=(j == 0), stop=(j == HT - 1)),
                         inc=("TE" if (c == 3 and j == HT - 1) else None))
            v_px = P.val("TE")
            for c in range(4):
                for j in range(HT):
                    P.op("te", lambda EE, wz=wz, j=j, c=c:
                         EE.matmul(psB[:, c * 512:(c + 1) * 512],
                                   wz[:, j * 128:(j + 1) * 128],
                                   xnT16[:, j * T + c * 512: j * T + (c + 1) * 512],
                                   start=(j == 0), stop=(j == HT - 1)),
                         inc=("TE" if (c == 3 and j == HT - 1) else None))
            v_pair = P.val("TE")
            win_ring[sx][1] = v_pair
            win_ring[sz_][1] = v_pair

            xb = af32[:, 2048:2048 + T + 3]
            t1 = af32[:, 6272:8320]
            t2 = af32[:, 8320:10368]
            u32 = af32[:, 2048:4096]
            sz32 = af32[:, 4224:6272]
            P.wait("ve", "TE", v_px)
            if k == 0:
                P.wait("ve", "SE", v_dt_se_all)
            P.op("ve", lambda EE, xb=xb: EE.memset(xb[:, 0:3], 0.0))
            P.op("ve", lambda EE, xb=xb: EE.tensor_copy(xb[:, 3:3 + T], psA[:]))
            P.op("ve", lambda EE, t1=t1, xb=xb, k=k:
                 EE.tensor_single_scalar(t1, xb[:, 0:T], c_cvw[:, 4 * k:4 * k + 1], ALU.mult))
            P.op("ve", lambda EE, t2=t2, xb=xb, t1=t1, k=k:
                 EE.scalar_tensor_tensor(t2, xb[:, 1:1 + T], c_cvw[:, 4 * k + 1:4 * k + 2],
                                         t1, ALU.mult, ALU.add))
            P.op("ve", lambda EE, t1=t1, xb=xb, t2=t2, k=k:
                 EE.scalar_tensor_tensor(t1, xb[:, 2:2 + T], c_cvw[:, 4 * k + 2:4 * k + 3],
                                         t2, ALU.mult, ALU.add))
            P.op("ve", lambda EE, t2=t2, xb=xb, t1=t1, k=k:
                 EE.scalar_tensor_tensor(t2, xb[:, 3:3 + T], c_cvw[:, 4 * k + 3:4 * k + 4],
                                         t1, ALU.mult, ALU.add), inc="VE")
            pair_conv.append(P.val("VE"))
            P.wait("se", "VE", pair_conv[-1])
            P.wait("se", "TE", v_pair)
            P.op("se", lambda EE, u32=u32, t2=t2, k=k:
                 EE.activation(u32, t2, AF.Silu, bias=c_cvb[:, k:k + 1]), inc="SE")
            P.op("se", lambda EE, sz32=sz32: EE.activation(sz32, psB[:], AF.Silu), inc="SE")
            pair_silu.append(P.val("SE"))

            dt16r = SL(3 + k % 2)
            P.wdma("sy", dt_spill[k])
            if k >= 2:
                P.wait("sy", "VE", pair_ve[k - 2])
            v_dtr = P.dma("sy", dt16r, s_dtdu[k][:, 0:T], "DR%d" % (k % 2))
            du16 = SL(5 + k % 2)
            sz16 = SL(7)
            w16 = SL(8)
            P.wait("ve", "SE", pair_silu[-1])
            P.wdma("ve", v_dtr)
            if k >= 1:
                P.wdma("ve", pair_spill[k - 1][0])
                P.wdma("ve", pair_spill[k - 1][1])
            P.op("ve", lambda EE, du16=du16, dt16r=dt16r, u32=u32:
                 EE.tensor_tensor(du16, dt16r, u32, ALU.mult))
            P.op("ve", lambda EE, sz16=sz16, sz32=sz32: EE.tensor_copy(sz16, sz32))
            P.op("ve", lambda EE, t1=t1, u32=u32, k=k:
                 EE.tensor_single_scalar(t1, u32, c_dcol[:, k:k + 1], ALU.mult))
            P.op("ve", lambda EE, w16=w16, t1=t1, sz32=sz32:
                 EE.tensor_tensor(w16, t1, sz32, ALU.mult), inc="VE")
            pair_ve.append(P.val("VE"))
            P.wait("sy", "VE", pair_ve[-1])
            P.dma("sy", s_dtdu[k][:, T:2 * T], du16, "US%d" % (k % 2))
            v_szst = P.dma("sy", s_sz[k], sz16, "SS")
            v_wst = P.dma("sy", s_w[k], w16, "WS%d" % (k % 2))
            pair_spill.append((v_szst, v_wst))

        v_p1_ve = P.val("VE")
        v_p1_se = P.val("SE")
        v_p1_te = P.val("TE")
        p1_spill_toks = [P.tok(s) for s in
                         ("DS0", "DS1", "US0", "US1", "SS", "WS0", "WS1", "DR0", "DR1")]

        part = af32[:, 0:TT * H]
        v_p1_ve = P.val("VE")
        v_p1_se = P.val("SE")
        v_p1_te = P.val("TE")

        moe_state = {
            "wgu_rel": [v_p1_te] * 2,
            "wd_rel": v_p1_te,
            "wd_tok": None,
            "hact_rel": [v_p1_te] * G_PAIRS,
            "brel": [0, 0, 0, 0],
            "gel_rel": [0, 0],
            "first": True,
        }

        def moe_track():
            st = moe_state
            wslab = 0
            for e in range(E):
                for g in range(N_GROUPS):
                    if e == 0 and g == 0:
                        P.wait("gp", "VE", v_p1_ve)
                    for fj in range(G_PAIRS):
                        pr = g * G_PAIRS + fj
                        ps_ = wslab % 2
                        wslab += 1
                        P.wait("gp", "TE", st["wgu_rel"][ps_])
                        pslot = marena[:, ps_ * 2048:(ps_ + 1) * 2048]
                        vg = P.dma("gp", pslot.rearrange("p (u j m) -> p u j m", u=2, j=HT),
                                   d_wguT[e, pr].rearrange("u (j p) m -> p u j m", p=128),
                                   "GU%d" % ps_)
                        vu = vg
                        wg = pslot[:, 0:1024]
                        wu = pslot[:, 1024:2048]
                        if fj == 1:
                            P.wait("gp", "TE", st["wd_rel"])
                            for q4 in range(2):
                                st["wd_tok"] = P.dma(
                                    "gp", wdb[:, q4 * 4096:(q4 + 1) * 4096].rearrange(
                                        "p (f h) -> p f h", f=4),
                                    d_wdT[e, g * G_PAIRS + q4 * 4:g * G_PAIRS + (q4 + 1) * 4]
                                    .rearrange("f p h -> p f h"), "WDL")
                        hrow = warena[:, fj * T:(fj + 1) * T]
                        for c in range(4):
                            pc = c % 2
                            psg = psB[:, pc * 512:(pc + 1) * 512]
                            psu = psB[:, (2 + pc) * 512:(3 + pc) * 512]
                            P.wdma("te", vg)
                            P.wdma("te", vu)
                            if st["first"]:
                                P.wait("te", "SE", v_p1_se)
                                P.wait("te", "VE", v_p1_ve)
                                st["first"] = False
                            P.wait("te", "VE", st["brel"][pc])
                            P.wait("te", "VE", st["brel"][2 + pc])
                            for j in range(HT):
                                P.op("te", lambda EE, psg=psg, wg=wg, j=j, c=c:
                                     EE.matmul(psg, wg[:, j * 128:(j + 1) * 128],
                                               xnT16[:, j * T + c * 512: j * T + (c + 1) * 512],
                                               start=(j == 0), stop=(j == HT - 1)),
                                     inc=("TE" if j == HT - 1 else None))
                            v_g = P.val("TE")
                            for j in range(HT):
                                P.op("te", lambda EE, psu=psu, wu=wu, j=j, c=c:
                                     EE.matmul(psu, wu[:, j * 128:(j + 1) * 128],
                                               xnT16[:, j * T + c * 512: j * T + (c + 1) * 512],
                                               start=(j == 0), stop=(j == HT - 1)),
                                     inc=("TE" if j == HT - 1 else None))
                            v_u = P.val("TE")
                            gelb = gelb16[:, pc * 512:(pc + 1) * 512]
                            P.wait("se", "TE", v_g)
                            P.wait("se", "VE", st["gel_rel"][pc])
                            P.op("se", lambda EE, gelb=gelb, psg=psg:
                                 EE.activation(gelb, psg, AF.Gelu), inc="SE")
                            v_gel = P.val("SE")
                            P.wait("ve", "SE", v_gel)
                            P.wait("ve", "TE", v_u)
                            if c == 0:
                                P.wait("ve", "TE", st["hact_rel"][fj])
                            P.op("ve", lambda EE, hrow=hrow, psu=psu, gelb=gelb, c=c:
                                 EE.tensor_tensor(hrow[:, c * 512:(c + 1) * 512], psu, gelb,
                                                  ALU.mult), inc="VE")
                            st["brel"][pc] = P.val("VE")
                            st["brel"][2 + pc] = P.val("VE")
                            st["gel_rel"][pc] = P.val("VE")
                            yield
                        st["wgu_rel"][ps_] = P.val("TE")
                    v_hact_all = P.val("VE")

                    P.wdma("te", st["wd_tok"])
                    P.wait("te", "VE", v_hact_all)
                    for i in range(TT):
                        for hh in range(2):
                            idx = i * 2 + hh
                            psd = psB[:, (idx % 2) * 512:(idx % 2 + 1) * 512]
                            P.wait("te", "VE", st["brel"][idx % 2])
                            for fj in range(G_PAIRS):
                                P.op("te", lambda EE, psd=psd, fj=fj, i=i, hh=hh:
                                     EE.matmul(psd,
                                               warena[:, fj * T + i * 128: fj * T + (i + 1) * 128],
                                               wdb[:, fj * 1024 + hh * 512: fj * 1024 + (hh + 1) * 512],
                                               start=(fj == 0), stop=(fj == G_PAIRS - 1)),
                                     inc=("TE" if fj == G_PAIRS - 1 else None))
                            v_dn = P.val("TE")
                            pv = part[:, i * H + hh * 512: i * H + (hh + 1) * 512]
                            P.wait("ve", "TE", v_dn)
                            P.op("ve", lambda EE, pv=pv, psd=psd, i=i, e=e:
                                 EE.scalar_tensor_tensor(pv, psd[:], c_cmb16[:, i * E + e:i * E + e + 1],
                                                         pv, ALU.mult, ALU.add), inc="VE")
                            st["brel"][idx % 2] = P.val("VE")
                            if hh == 1:
                                yield
                    v_grp_te = P.val("TE")
                    moe_state["wd_rel"] = v_grp_te
                    for fj in range(G_PAIRS):
                        moe_state["hact_rel"][fj] = v_grp_te
            while True:
                yield None

        moe_gen = moe_track()

        def pull_moe(n):
            for _ in range(n):
                next(moe_gen)

        p1_spill_toks = [P.tok(s) for s in
                         ("DS0", "DS1", "US0", "US1", "SS", "WS0", "WS1", "DR0", "DR1")]
        for tk in p1_spill_toks:
            P.wdma("sy", tk)
        P.wait("sy", "TE", v_p1_te)
        P.wait("sy", "VE", v_p1_ve)
        P.wait("sy", "SE", v_p1_se)

        P.wait("ve", "SE", v_p1_se)
        P.op("ve", lambda EE: EE.memset(part, 0.0), inc="VE")

        scan_se = [None] * NSC
        scan_mid = [None] * NSC
        scan_ve = [None] * NSC
        scan_te = [None] * NSC
        ld = [None, None]
        yev = []

        def issue_sd(k2q):
            sq = k2q % 2
            if k2q >= 2:
                P.wait("sy", "VE", scan_mid[k2q - 2])
            Dq, jq = divmod(k2q, 16)
            ld[sq] = P.dma("sy", abf[:, 2 * sq * T:(2 * sq + 2) * T],
                           s_dtdu[Dq][jq * 8:(jq + 1) * 8].unsqueeze(1)
                           .broadcast_to([8, 16, 2 * T]), "SD%d" % sq)

        def emit_sel(k2):
            D, j = divmod(k2, 16)
            hc_t = SL(6 + k2 % 2)
            P.wait("te", "VE", scan_ve[k2])
            if j == 0 and D >= 1:
                P.wait("te", "VE", yev[D - 1][0])
            selv = c_sel[:, j * 128:(j + 1) * 128]
            for c in range(4):
                P.op("te", lambda EE, selv=selv, hc_t=hc_t, c=c, j=j:
                     EE.matmul(psA[:, c * 512:(c + 1) * 512], selv,
                               hc_t[:, c * 512:(c + 1) * 512],
                               start=(j == 0), stop=(j == 15)),
                     inc=("TE" if c == 3 else None))
            scan_te[k2] = P.val("TE")

        def emit_epi(D):
            k2l = D * 16 + 15
            szt = SL(8)
            wt = SL(4)
            y16 = SL(8)
            tfp = SL(5)
            if D >= 1:
                P.wait("sy", "VE", yev[D - 1][0])
                P.wdma("sy", yev[D - 1][1])
            v_szl = P.dma("sy", szt, s_sz[D], "SZL")
            k2cur = (D + 1) * 16 if D < 15 else NSC - 1
            P.wait("sy", "VE", scan_mid[k2cur])
            v_wl = P.dma("sy", wt, s_w[D], "SWL")
            P.wait("ve", "TE", scan_te[k2l])
            P.wdma("ve", v_szl)
            P.wdma("ve", v_wl)
            if D >= 1:
                P.wdma("ve", yev[D - 1][1])
            P.op("ve", lambda EE, tfp=tfp, szt=szt:
                 EE.tensor_tensor(tfp, psA[:], szt, ALU.mult))
            P.op("ve", lambda EE, y16=y16, tfp=tfp, wt=wt:
                 EE.tensor_tensor(y16, tfp, wt, ALU.add), inc="VE")
            v_yev = P.val("VE")
            yev.append((v_yev, None))
            P.wait("sy", "VE", v_yev)
            v_yst = P.dma("sy", s_y[D], y16, "YS")
            yev[-1] = (v_yev, v_yst)

        issue_sd(0)
        issue_sd(1)
        for k2 in range(NSC):
            sl = k2 % 2
            D, j = divmod(k2, 16)
            dt_h = SL(2 * sl)
            du_h = SL(2 * sl + 1)
            b_in = SL(4)
            h_t = SL(5)
            hc_t = SL(6 + sl)
            if k2 >= 1:
                emit_sel(k2 - 1)
            P.wdma("se", ld[sl])
            P.op("se", lambda EE, dt_h=dt_h, k2=k2:
                 EE.activation(dt_h, dt_h, AF.Exp, scale=c_acols[:, k2:k2 + 1]), inc="SE")
            scan_se[k2] = P.val("SE")
            pull_moe(1)
            P.wdma("ve", ld[sl])
            if k2 == 0:
                P.wdma("ve", v_ch)
            P.wait("ve", "SE", scan_se[k2])
            P.op("ve", lambda EE, b_in=b_in, du_h=du_h:
                 EE.tensor_tensor(b_in, du_h, bhh[:], ALU.mult))
            pull_moe(1)
            P.op("ve", lambda EE, h_t=h_t, dt_h=dt_h, b_in=b_in:
                 EE.tensor_tensor_scan(h_t, dt_h, b_in, 0.0, ALU.mult, ALU.add), inc="VE")
            scan_mid[k2] = P.val("VE")
            if sl == 1 and k2 + 1 < NSC:
                issue_sd(k2 + 1)
                if k2 + 2 < NSC:
                    issue_sd(k2 + 2)
            pull_moe(1)
            if k2 >= 2:
                P.wait("ve", "TE", scan_te[k2 - 2])
            P.op("ve", lambda EE, hc_t=hc_t, h_t=h_t:
                 EE.tensor_tensor(hc_t, h_t, chh[:], ALU.mult), inc="VE")
            scan_ve[k2] = P.val("VE")
            pull_moe(1)
            if j == 0 and D >= 1:
                emit_epi(D - 1)
        emit_sel(NSC - 1)
        emit_epi(15)

        v_p2_ve = P.val("VE")
        v_p2_se = P.val("SE")
        v_p2_te = P.val("TE")
        p2_toks = [P.tok(s) for s in ("SD0", "SD1", "SU0", "SU1", "SZL", "SWL", "YS")]

        wout_sb = abf[:, 1 * T:9 * T]
        pre = bhh[:].bitcast(F32)
        scv = chh[:].bitcast(F32)
        xv3 = c_sel[:].bitcast(F32)
        P.wait("gp", "VE", v_p2_ve)
        P.wait("gp", "SE", v_p2_se)
        P.wait("gp", "TE", v_p2_te)
        for tk in p2_toks:
            P.wdma("gp", tk)
        v_wout = P.dma("gp", wout_sb.rearrange("p (k h) -> p k h", k=DIT),
                       d_woutT.rearrange("k p h -> p k h"), "WO")

        pre_ve = []
        ep_se = []
        tile_ve = []
        ymm_te = []
        for i in range(TT):
            ybuf = SL(0)
            if i == 0:
                P.wait("sy", "SE", v_p2_se)
                P.wait("sy", "VE", v_p2_ve)
                P.wait("sy", "TE", v_p2_te)
            else:
                P.wait("sy", "TE", ymm_te[i - 1])
            if i == 0:
                P.wdma("sy", P.tok("YS"))
            yld = None
            for D in range(DIT):
                yld = P.dma("sy", ybuf[:, D * 128:(D + 1) * 128],
                            s_y[D][:, i * 128:(i + 1) * 128], "YL")
            if len(pre_ve) >= 2:
                P.wait("sy", "VE", pre_ve[-1])
            vx2 = P.dma("sy", xv3, d_x.rearrange("(i p) h -> i p h", p=128)[i], "XV")
            for hh in range(2):
                idx = i * 2 + hh
                pso = psA[:, (idx % 4) * 512:((idx % 4) + 1) * 512]
                P.wdma("te", yld)
                if i == 0 and hh == 0:
                    P.wdma("te", v_wout)
                    P.wait("te", "VE", v_p2_ve)
                if idx >= 4:
                    P.wait("te", "VE", pre_ve[idx - 4])
                for D in range(DIT):
                    P.op("te", lambda EE, pso=pso, ybuf=ybuf, D=D, hh=hh:
                         EE.matmul(pso, ybuf[:, D * 128:(D + 1) * 128],
                                   wout_sb[:, D * H + hh * 512: D * H + (hh + 1) * 512],
                                   start=(D == 0), stop=(D == DIT - 1)),
                         inc=("TE" if D == DIT - 1 else None))
                v_wo = P.val("TE")
                P.wait("ve", "TE", v_wo)
                P.wdma("ve", vx2)
                P.op("ve", lambda EE, pso=pso, hh=hh, i=i:
                     EE.scalar_tensor_tensor(pre[:, hh * 512:(hh + 1) * 512],
                                             xv3[:, hh * 512:(hh + 1) * 512],
                                             c_rstd[:, i * 8:i * 8 + 1], pso,
                                             ALU.mult, ALU.add), inc="VE")
                pre_ve.append(P.val("VE"))
            ymm_te.append(P.val("TE"))
            s_sum = sm8[:, 0:1]
            s_ssq = sm8[:, 1:2]
            mu8b = sm8[:, 8:16]
            q8b = sm8[:, 16:24]
            m28b = sm8[:, 24:32]
            ln8b = sm8[:, 32:40]
            rs8b = sm8[:, 40:48]
            v8b = sm8[:, 48:56]
            P.op("ve", lambda EE: EE.reduce_sum(s_sum, pre, axis=AX), inc="VE")
            P.wait("se", "VE", P.val("VE"))
            P.op("se", lambda EE: EE.activation(scv, pre, AF.Square, accum_out=s_ssq),
                 inc="SE")
            P.wait("ve", "SE", P.val("SE"))
            P.op("ve", lambda EE:
                 EE.tensor_scalar(mu8b, smE[:, 0:8], s_sum, 1.0 / H, ALU.mult, ALU.mult))
            P.op("ve", lambda EE:
                 EE.tensor_scalar(q8b, smE[:, 0:8], s_ssq, 1.0 / H, ALU.mult, ALU.mult), inc="VE")
            P.sop("ve", lambda EE: EE.tensor_tensor(m28b, mu8b, mu8b, ALU.mult))
            P.sop("ve", lambda EE: EE.tensor_tensor(v8b, q8b, m28b, ALU.subtract))
            P.wait("se", "VE", P.val("VE"))
            P.sop("se", lambda EE: EE.activation(ln8b, v8b, AF.Ln, bias=smE[:, 24:25]))
            P.sop("se", lambda EE: EE.activation(rs8b, ln8b, AF.Exp, scale=smE[:, 32:33]))
            P.wait("ve", "SE", P.val("SE"))
            P.op("ve", lambda EE:
                 EE.tensor_scalar(m28b, mu8b, rs8b[:, 0:1], -1.0, ALU.mult, ALU.mult), inc="VE")
            P.wait("se", "VE", P.val("VE"))
            P.op("se", lambda EE:
                 EE.activation(scv, pre, AF.Identity, scale=rs8b[:, 0:1], bias=m28b[:, 0:1]),
                 inc="SE")
            v_norm = P.val("SE")
            pv = part[:, i * H:(i + 1) * H]
            P.wait("ve", "SE", v_norm)
            if i == 0:
                P.wdma("ve", v_c2)
            P.op("ve", lambda EE: EE.tensor_tensor(pre, scv, c_lng[:], ALU.mult))
            P.op("ve", lambda EE, pv=pv:
                 EE.scalar_tensor_tensor(pv, pre, 1.0, pv, ALU.mult, ALU.add), inc="VE")
            tile_ve.append(P.val("VE"))
            pull_moe(10)

        pull_moe(3000)
        v_all_ve = P.val("VE")
        P.wait("gp", "TE", P.val("TE"))
        v_lnb5 = P.dma("gp", c_lnb, d_lnb, "C2")
        P.wait("sy", "VE", v_all_ve)
        P.wait("sy", "TE", P.val("TE"))
        st_ve = []
        for i in range(TT):
            x5 = abf[:, (i % 2) * T:(i % 2) * T + T].bitcast(F32)
            if i >= 2:
                P.wait("sy", "VE", st_ve[i - 2])
            vx5 = P.dma("sy", x5, d_x.rearrange("(i p) h -> i p h", p=128)[i], "X5%d" % (i % 2))
            pv = part[:, i * H:(i + 1) * H]
            P.wdma("ve", vx5)
            if i == 0:
                P.wdma("ve", v_lnb5)
            P.op("ve", lambda EE, pv=pv, x5=x5:
                 EE.scalar_tensor_tensor(pv, x5, 1.0, pv, ALU.mult, ALU.add))
            P.op("ve", lambda EE, pv=pv:
                 EE.scalar_tensor_tensor(pv, c_lnb[:], 1.0, pv, ALU.mult, ALU.add), inc="VE")
            st_ve.append(P.val("VE"))
            P.wait("sy", "VE", st_ve[-1])
            P.dma("sy", d_out.rearrange("(i p) h -> i p h", p=128)[i],
                  part[:, i * H:(i + 1) * H], "OS")
        P.wdma("sy", P.tok("OS"))

        @blk.sync
        def _(EE):
            for f in P.q["sy"]:
                f(EE)

        @blk.gpsimd
        def _(EE):
            for f in P.q["gp"]:
                f(EE)

        @blk.vector
        def _(EE):
            for f in P.q["ve"]:
                f(EE)

        @blk.scalar
        def _(EE):
            for f in P.q["se"]:
                f(EE)

        @blk.tensor
        def _(EE):
            for f in P.q["te"]:
                f(EE)

    return nc


def _host_layout(inputs):
    c = np.ascontiguousarray
    W_in = np.asarray(inputs["L0_attn_W_in"], np.float32)
    W_dt = np.asarray(inputs["L0_attn_W_dt"], np.float32)
    b_dt = np.asarray(inputs["L0_attn_b_dt"], np.float32)
    A_log = np.asarray(inputs["L0_attn_A_log"], np.float32)
    Dv = np.asarray(inputs["L0_attn_D"], np.float32)
    conv_w = np.asarray(inputs["L0_attn_conv_w"], np.float32)
    conv_b = np.asarray(inputs["L0_attn_conv_b"], np.float32)
    W_out = np.asarray(inputs["L0_attn_W_out"], np.float32)
    ln_g = np.asarray(inputs["L0_attn_ln_g"], np.float32)
    ln_b = np.asarray(inputs["L0_attn_ln_b"], np.float32)
    gate_w = np.asarray(inputs["L0_ffn_gate_w"], np.float32)
    Wgu = np.asarray(inputs["L0_ffn_gate_up_w"], np.float32)
    Wd = np.asarray(inputs["L0_ffn_down_w"], np.float32)

    d = {}
    W_in_p = np.zeros((33 * 128, H), np.float32)
    W_in_p[:4192] = W_in
    d["winT"] = c(W_in_p.reshape(33, 128, H).transpose(0, 2, 1))
    d["wdtT"] = c(W_dt.reshape(DIT, 128, RK).transpose(0, 2, 1))
    wgt = Wgu.reshape(E, 64, 128, H).transpose(0, 1, 3, 2)
    d["wguT"] = c(np.stack([wgt[:, :32], wgt[:, 32:]], axis=2))
    d["wdT"] = c(Wd.reshape(E, H, 32, 128).transpose(0, 2, 3, 1))
    d["woutT"] = c(W_out.reshape(H, 16, 128).transpose(1, 2, 0))
    d["gwT"] = c(gate_w.reshape(E, HT, 128).transpose(1, 2, 0))
    d["alogr"] = c(A_log.reshape(NSC, 8, NST).transpose(1, 2, 0).reshape(128, NSC))
    d["bdt"] = c(b_dt.reshape(DIT, 128).T)
    d["dcol"] = c(Dv.reshape(DIT, 128).T)
    d["cvw"] = c(conv_w.reshape(16, 128, 4).transpose(1, 0, 2).reshape(128, 64))
    d["cvb"] = c(conv_b.reshape(16, 128).T)
    d["lng"] = c(np.broadcast_to(ln_g, (128, H)))
    d["lnb"] = c(np.broadcast_to(ln_b, (128, H)))
    d["identf"] = np.eye(128, dtype=np.float32)
    sel = np.zeros((128, 16, 128), np.float32)
    p_idx = np.arange(128)
    for j in range(16):
        sel[p_idx, j, j * 8 + p_idx // 16] = 1.0
    d["sel"] = np.ascontiguousarray(sel.reshape(128, 2048)).astype(ml_dtypes.bfloat16)
    return d


_CACHE = {}


def kernel(**inputs):
    if "nc" not in _CACHE:
        _CACHE["nc"] = build_nc()
    nc = _CACHE["nc"]
    shared = _host_layout(inputs)
    x_full = np.ascontiguousarray(np.asarray(inputs["L0_attn_x"], np.float32))
    in_maps = []
    for core in range(8):
        m = dict(shared)
        m["x"] = np.ascontiguousarray(x_full[core])
        in_maps.append(m)
    res = run_bass_kernel_spmd(nc, in_maps, core_ids=list(range(8)))
    return np.stack([np.asarray(res.results[c]["out"]) for c in range(8)], axis=0)
